# version 06
# baseline (speedup 1.0000x reference)
import math
import numpy as np
import ml_dtypes
from contextlib import ExitStack

import concourse.bass as bass
import concourse.mybir as mybir
from concourse.bass import IndirectOffsetOnAxis
from concourse.bass_utils import run_bass_kernel_spmd

F32 = mybir.dt.float32
BF16 = mybir.dt.bfloat16
I32 = mybir.dt.int32
AF = mybir.ActivationFunctionType
ALU = mybir.AluOpType
AX = mybir.AxisListType

NCORES = 8
B = 4
S = 4096
DM = 2048
TLOC = 2048
NT = 16
NQC = 4
QC = 512
NH = 16
NKV = 8
DH = 128
FE = 64
E = 8
CAP = 2560
CAPP = 2688
NGT = 21
AG2R = 2816
ZROW = 2688
MAGIC = 1.5 * 2.0 ** 23
INV2PI = 1.0 / (2.0 * math.pi)
TWOPI = 2.0 * math.pi
LN_EPS = 1e-5
SCALE_B = 1.0 / math.sqrt(DH)
ENGINES = ("sync", "tensor", "scalar", "vector", "gpsimd")
UPTO = 99
SHARED_AG = True


class Sem:
    def __init__(self, nc, es, name):
        self.h = es.enter_context(nc.semaphore(name))
        self.n = 0


class _Rec:

    def __init__(self, plan, eng, selfsem=None):
        self._p = plan
        self._e = eng
        self._ss = selfsem

    def wait(self, sem, val):
        if val > 0:
            self._p.ops[self._e].append(("wait", sem, val))

    def inc(self, sem, k=1):
        if sem is self._ss:
            return sem.n
        self._p.ops[self._e].append(("inc", sem, k))
        sem.n += k
        return sem.n

    DMA_OPS = {"dma_start", "indirect_dma_start", "collective_compute",
               "dma_start_transpose"}

    def op(self, name, *args, **kw):
        if self._ss is not None and name not in self.DMA_OPS:
            if self._ss.n > 0:
                self._p.ops[self._e].append(("wait", self._ss, self._ss.n))
            self._p.ops[self._e].append(("op", name, args, kw))
            self._p.ops[self._e].append(("inc", self._ss, 1))
            self._ss.n += 1
        else:
            self._p.ops[self._e].append(("op", name, args, kw))

    def __getattr__(self, name):
        def f(*args, **kw):
            self.op(name, *args, **kw)
        return f


class Plan:
    def __init__(self, k):
        self.k = k
        self.ops = {e: [] for e in ENGINES}
        ss = {"vector": k.sVEC, "scalar": k.sACT, "gpsimd": k.sGP}
        self.r = {e: _Rec(self, e, ss.get(e)) for e in ENGINES}

    def eng(self, name):
        return self.r[name]

    def barrier(self):
        k = self.k
        sems = [s for s in k.all_sems if s.n > 0]
        for e in ENGINES:
            for s in sems:
                self.r[e].wait(s, s.n)

    def run(self, nc, name):
        with nc.Block(name) as blk:
            for eng in ENGINES:
                ops = self.ops[eng]
                if not ops:
                    continue

                def body(e, ops=ops):
                    last = None
                    for item in ops:
                        if item[0] == "wait":
                            e.wait_ge(item[1].h, item[2])
                        elif item[0] == "inc":
                            assert last is not None
                            last.then_inc(item[1].h, item[2])
                        else:
                            _, nm, args, kw = item
                            last = getattr(e, nm)(*args, **kw)

                getattr(blk, eng)(body)


def _bf(x):
    return np.asarray(x, dtype=ml_dtypes.bfloat16)


def _core_tables(core):
    par = core % 2
    b = core // 2
    loc = np.arange(TLOC)
    j = loc // 128
    pos = (2 * j + par) * 128 + (loc % 128)
    inv_freq = 1.0 / (10000.0 ** (np.arange(0, DH, 2, dtype=np.float64) / DH))
    ang = pos[None, :].astype(np.float64) * inv_freq[:, None]
    cs = np.zeros((128, 2 * TLOC), dtype=np.float32)
    cs[0:64, 0:TLOC] = np.cos(ang); cs[64:128, 0:TLOC] = np.cos(ang)
    cs[0:64, TLOC:] = np.sin(ang); cs[64:128, TLOC:] = np.sin(ang)

    i = np.arange(128)
    tri = (i[:, None] <= i[None, :]).astype(np.float32)
    if par == 0:
        m1, m2 = tri, np.zeros((128, 128), np.float32)
    else:
        m1, m2 = np.ones((128, 128), np.float32), tri

    gidx = np.zeros((128, NT), dtype=np.float32)
    for jj in range(NT):
        l = jj * 128 + np.arange(128)
        gidx[:, jj] = (l // 512) * (NCORES * 512) + core * 512 + (l % 512)

    trimy = np.zeros((128, NT), dtype=np.float32)
    for jj in range(NT):
        gmine = b * 32 + 2 * jj + par
        for r in range(128):
            coreR, jR = r // 16, r % 16
            gR = (coreR // 2) * 32 + 2 * jR + (coreR % 2)
            if gR < gmine:
                trimy[r, jj] = 1.0

    idxscr = np.zeros((128, NT), dtype=np.int32)
    for jj in range(NT):
        for ii in range(128):
            gg = 16 * par + ii // 8
            r = gg % 2
            idxscr[ii, jj] = (r * 16 + jj) * 128 + (gg // 2) * 8 + (ii % 8)
    trimyr = np.repeat(trimy.reshape(128, NT, 1), 128, axis=2).reshape(128, NT * 128)
    return dict(cos_sin=cs, m1=_bf(m1), m2=_bf(m2), gidx=gidx, trimy=_bf(trimy),
                idxscr=idxscr, trimyr=_bf(trimyr))


def _const_tables():
    i = np.arange(128)
    ident = _bf(np.eye(128, dtype=np.float32))
    tri_incl = _bf((i[:, None] <= i[None, :]).astype(np.float32))
    c255 = np.zeros((128, 255), dtype=np.float32)
    c255[:, 127] = 1.0
    iota_e = np.tile(np.arange(E, dtype=np.float32)[None, :], (128, 1))
    constp = np.zeros((128, 8), dtype=np.float32)
    constp[:, 0] = LN_EPS
    constp[:, 1] = 1e-6
    constp[:, 2] = -1e9
    constp[:, 3] = 12544.0
    prot = np.zeros((128, 128), dtype=np.float32)
    for m in range(64):
        prot[m + 64, m] = -1.0
        prot[m, m + 64] = 1.0
    return dict(ident=ident, tri_incl=tri_incl, c255=_bf(c255), iota_e=iota_e,
                constp=constp, prot=_bf(prot),
                ones_col=_bf(np.ones((128, 1), np.float32)),
                ones_row=_bf(np.ones((1, 128), np.float32)),
                ones512=_bf(np.ones((1, 512), np.float32)),
                neg8=_bf(-np.ones((1, 8), np.float32)))


def _prepare_inputs(inp):
    x = np.asarray(inp["L0_attn_a_hidden_states"], dtype=np.float32)
    co = _const_tables()
    f32c = lambda a: np.ascontiguousarray(np.asarray(a), dtype=np.float32)
    maps = []
    for core in range(NCORES):
        par, b = core % 2, core // 2
        tiles = x[b].reshape(32, 128, DM)[par::2]
        xT = np.ascontiguousarray(tiles.reshape(TLOC, DM).T)
        t = _core_tables(core)
        m = dict(
            xT=xT,
            WqT=f32c(np.asarray(inp["L0_attn_a_Wq"]).T),
            WkT=f32c(np.asarray(inp["L0_attn_a_Wk"]).T),
            WvT=f32c(np.asarray(inp["L0_attn_a_Wv"]).T),
            WoT=f32c(np.asarray(inp["L0_attn_a_Wo"]).T),
            bq=f32c(np.asarray(inp["L0_attn_a_bq"]).reshape(1, 2048)),
            bk=f32c(np.asarray(inp["L0_attn_a_bk"]).reshape(1, 1024)),
            bv=f32c(np.asarray(inp["L0_attn_a_bv"]).reshape(1, NKV * 128)),
            bo=f32c(np.asarray(inp["L0_attn_a_bo"]).reshape(1, 2048)),
            RT=f32c(np.asarray(inp["L0_attn_a_rand_matrix"]).T),
            Wqkv=f32c(inp["L0_attn_b_w_qkv"]),
            bqkv=f32c(np.asarray(inp["L0_attn_b_b_qkv"])[:4096].reshape(1, 4096)),
            bvb=f32c(np.asarray(inp["L0_attn_b_b_qkv"])[4096:].reshape(1, 2048)),
            Wout=f32c(inp["L0_attn_b_w_out"]),
            bout=f32c(np.asarray(inp["L0_attn_b_b_out"]).reshape(1, 2048)),
            gT=f32c(np.asarray(inp["L0_ffn_gate_weight"]).T),
            W1=f32c(np.asarray(inp["L0_ffn_w1"])[core]),
            W2=f32c(np.asarray(inp["L0_ffn_w2"])[core]),
            cos_sin=t["cos_sin"], m1=t["m1"], m2=t["m2"], gidx=t["gidx"],
            idxscr=t["idxscr"], trimyr=t["trimyr"],
            trimy=t["trimy"], ident=co["ident"], tri_incl=co["tri_incl"],
            c255=co["c255"], iota_e=co["iota_e"], ones_col=co["ones_col"],
            ones_row=co["ones_row"], ones512=co["ones512"], neg8=co["neg8"],
            constp=co["constp"], prot=co["prot"],
        )
        maps.append(m)
    return maps


def _assemble(results):
    out = np.empty((B, S, DM), dtype=np.float32)
    for core in range(NCORES):
        par, b = core % 2, core // 2
        xT = np.asarray(results[core]["out"])
        out[b].reshape(32, 128, DM)[par::2] = xT.T.reshape(NT, 128, DM)
    return out


class K:
    def __init__(self, upto=UPTO):
        self.upto = upto
        nc = bass.Bass("TRN2", target_bir_lowering=False,
                       dynamic_dma_scratch_size=16384)
        self.nc = nc
        self.es = ExitStack()
        d = self.d = {}

        def din(name, shape, dtype=F32):
            d[name] = nc.dram_tensor(name, shape, dtype, kind="ExternalInput").ap()

        din("xT", [DM, TLOC])
        din("WqT", [DM, 2048]); din("WkT", [DM, 1024]); din("WvT", [DM, 1024])
        din("WoT", [DM, 2048])
        din("bq", [1, 2048]); din("bk", [1, 1024]); din("bv", [1, NKV * 128])
        din("bo", [1, 2048]); din("RT", [128, FE])
        din("Wqkv", [DM, 3 * DM]); din("bqkv", [1, 4096]); din("bvb", [1, 2048])
        din("Wout", [DM, DM]); din("bout", [1, 2048])
        din("gT", [DM, E])
        din("W1", [DM, DM]); din("W2", [DM, DM])
        din("cos_sin", [128, 2 * TLOC])
        din("m1", [128, 128], BF16); din("m2", [128, 128], BF16)
        din("gidx", [128, NT]); din("trimy", [128, NT], BF16)
        din("idxscr", [128, NT], I32)
        din("trimyr", [128, NT * 128], BF16)
        din("ident", [128, 128], BF16); din("tri_incl", [128, 128], BF16)
        din("c255", [128, 255], BF16); din("iota_e", [128, E])
        din("ones_col", [128, 1], BF16); din("ones_row", [1, 128], BF16)
        din("constp", [128, 8]); din("prot", [128, 128], BF16)
        din("ones512", [1, 512], BF16); din("neg8", [1, 8], BF16)

        self.out = nc.dram_tensor("out", [DM, TLOC], F32, kind="ExternalOutput").ap()
        self.out_r = self.out.rearrange("(k p) q -> p k q", p=128)

        def dint(name, shape, dtype, shared=False):
            kw = dict(addr_space="Shared") if shared else {}
            d[name] = nc.dram_tensor(name, shape, dtype, **kw).ap()

        dint("WqTb", [DM, 2048], BF16); dint("WkTb", [DM, 1024], BF16)
        dint("WvTb", [DM, 1024], BF16); dint("WoTb", [DM, 2048], BF16)
        dint("Wqkvb", [DM, 3 * DM], BF16); dint("Woutb", [DM, DM], BF16)
        dint("W1b", [DM, DM], BF16); dint("W2b", [DM, DM], BF16)
        dint("ag1in", [TLOC, DM], BF16)
        dint("ag1out", [NCORES * TLOC, DM], BF16, shared=SHARED_AG)
        dint("ag2in", [AG2R, DM], BF16)
        dint("ag2out", [NCORES * AG2R, DM], BF16, shared=SHARED_AG)
        dint("fragi", [E * CAPP, 1], F32); dint("fragw", [E * CAPP, 1], F32)
        dint("fragio", [CAPP, 1], F32); dint("fragwo", [CAPP, 1], F32)
        dint("cntin", [NT, E], F32); dint("cntout", [NCORES * NT, E], F32)
        dint("kvK", [DM, TLOC], BF16)
        dint("kvV", [TLOC, DM], BF16)
        dint("kvKg", [2 * DM, TLOC], BF16)
        dint("kvVg", [2 * TLOC, DM], BF16)
        dint("mkq", [32, 1], F32); dint("mkqg", [64, 1], F32)
        dint("qdram", [DM, TLOC], BF16)
        dint("ydram", [DM, TLOC], BF16)
        dint("mixh", [NH * TLOC, 128], BF16)
        dint("mixhg", [2 * NH * TLOC, 128], BF16)

        es = self.es
        self.sD = Sem(nc, es, "sD")
        self.sDS = Sem(nc, es, "sDS")
        self.sDG = Sem(nc, es, "sDG")
        self.sPE = Sem(nc, es, "sPE")
        self.sACT = Sem(nc, es, "sACT")
        self.sVEC = Sem(nc, es, "sVEC")
        self.sCC = Sem(nc, es, "sCC")
        self.sGP = Sem(nc, es, "sGP")
        self.sW = [Sem(nc, es, f"sW{i}") for i in range(4)]
        self.sG = [Sem(nc, es, f"sG{i}") for i in range(4)]
        self.sH = [Sem(nc, es, f"sH{i}") for i in range(3)]
        self.sCAST = Sem(nc, es, "sCAST")
        self.sCAST2 = Sem(nc, es, "sCAST2")
        self.sSC = Sem(nc, es, "sSC")
        self.all_sems = [self.sD, self.sPE, self.sACT,
                         self.sVEC, self.sCC, self.sGP, self.sSC] + self.sW + self.sG + self.sH

        sb = lambda name, shape, dt: es.enter_context(nc.sbuf_tensor(name, shape, dt))
        self.c_ident = sb("c_ident", [128, 128], BF16)
        self.c_tri = sb("c_tri", [128, 128], BF16)
        self.c_c255 = sb("c_c255", [128, 255], BF16)
        self.c_trimy = sb("c_trimy", [128, NT], BF16)
        self.c_m1 = sb("c_m1", [128, 128], BF16)
        self.c_m2 = sb("c_m2", [128, 128], BF16)
        self.c_iota = sb("c_iota", [128, E], F32)
        self.c_gidx = sb("c_gidx", [128, NT], F32)
        self.c_idxscr = sb("c_idxscr", [128, NT], I32)
        self.c_trimyr = sb("c_trimyr", [128, NT * 128], BF16)
        self.c_cri = sb("c_cri", [128, NT], I32)
        self.c_sci = sb("c_sci", [128, 2 * NGT + 2], I32)
        self.c_ones = sb("c_ones", [128, 1], BF16)
        self.c_onesr = sb("c_onesr", [1, 128], BF16)
        self.c_constp = sb("c_constp", [128, 8], F32)
        self.c_prot = sb("c_prot", [128, 128], BF16)
        self.c_csb = sb("c_csb", [128, 2 * TLOC], BF16)
        self.c_rt = sb("c_rt", [128, FE], BF16)
        self.c_gT = sb("c_gT", [128, NT * E], BF16)
        self.c_bq = sb("c_bq", [1, 2048], BF16)
        self.c_bk = sb("c_bk", [1, 1024], BF16)
        self.c_bo = sb("c_bo", [1, 2048], BF16)
        self.c_bv = sb("c_bv", [1, NKV * 128], BF16)
        self.c_bqkv = sb("c_bqkv", [1, 4096], BF16)
        self.c_bvb = sb("c_bvb", [1, 2048], BF16)
        self.c_bout = sb("c_bout", [1, 2048], BF16)
        self.c_o512 = sb("c_o512", [1, 512], BF16)
        self.c_nego = sb("c_nego", [1, 8], BF16)

        self.g1 = sb("g1", [128, 8192], F32)
        self.g2 = sb("g2", [128, 8192], F32)
        self.g3 = sb("g3", [128, 8192], BF16)
        self.g4 = sb("g4", [128, 8192], BF16)
        self.g5 = sb("g5", [128, 8192], BF16)
        self.g6 = sb("g6", [128, 4096], BF16)
        self.g7 = sb("g7", [128, 4096], BF16)
        self.g8 = sb("g8", [128, 3072], BF16)
        self.g9 = sb("g9", [128, 2048], BF16)
        self.g10 = sb("g10", [128, 1024], F32)
        self.g11 = sb("g11", [128, 1024], F32)
        self.g12 = sb("g12", [128, 1536], F32)
        self.g13 = sb("g13", [128, 1536], BF16)
        self.g14 = sb("g14", [128, 640], F32)
        self.g15 = sb("g15", [128, 2048], BF16)

        ps = lambda name, shape, dt: es.enter_context(nc.psum_tensor(name, shape, dt))
        self.P6 = [ps(f"P{i}", [128, 512], F32) for i in range(6)]
        self.PT = ps("PT", [128, 1024], BF16)
        self.P7 = ps("P7", [128, 512], F32)

        self.plans = []

    def build(self):
        with self.es:
            self.phase0()
            if self.upto >= 1:
                self.attn_a()
                self.attna_fin()
            if self.upto >= 2:
                self.ffn(0)
            if self.upto >= 3:
                self.attn_b()
            if self.upto >= 4:
                self.ffn(1)
            self.finish()
            import os
            nplans = int(os.environ.get("KNPLANS", "9999"))
            if os.environ.get("KSPLIT"):
                for name, P in self.plans[:nplans]:
                    P.run(self.nc, name)
            else:
                merged = Plan(self)
                for name, P in self.plans[:nplans]:
                    for e in ENGINES:
                        merged.ops[e].extend(P.ops[e])
                merged.run(self.nc, "mega")
        return self.nc

    def phase0(self):
        P = Plan(self)
        d = self.d
        SY, GP, V = P.eng("sync"), P.eng("gpsimd"), P.eng("vector")

        for tile, name in [
            (self.c_ident, "ident"), (self.c_tri, "tri_incl"),
            (self.c_c255, "c255"), (self.c_trimy, "trimy"), (self.c_m1, "m1"),
            (self.c_m2, "m2"), (self.c_iota, "iota_e"), (self.c_gidx, "gidx"),
            (self.c_ones, "ones_col"), (self.c_onesr, "ones_row"),
            (self.c_idxscr, "idxscr"), (self.c_trimyr, "trimyr"),
            (self.c_constp, "constp"), (self.c_prot, "prot"),
            (self.c_o512, "ones512"), (self.c_nego, "neg8"),
        ]:
            SY.dma_start(out=tile[:], in_=d[name]); SY.inc(self.sD, 16)

        for tile, name in [(self.c_csb, "cos_sin"), (self.c_rt, "RT"),
                           (self.c_bv, "bv"), (self.c_bvb, "bvb"),
                           (self.c_bq, "bq"), (self.c_bk, "bk"),
                           (self.c_bo, "bo"), (self.c_bqkv, "bqkv"),
                           (self.c_bout, "bout")]:
            GP.dma_start(out=tile[:], in_=d[name]); GP.inc(self.sDG, 16)
        GP.dma_start(out=self.c_gT.ap().rearrange("p (k e) -> p k e", e=E),
                     in_=d["gT"].rearrange("(k p) e -> p k e", p=128))
        GP.inc(self.sDG, 16)
        for s_, dst in [("WqT", "WqTb"), ("WkT", "WkTb"), ("WvT", "WvTb")]:
            GP.dma_start(out=d[dst], in_=d[s_]); GP.inc(self.sCAST, 16)
        early = self.sCAST.n
        for s_, dst in [("WoT", "WoTb"), ("Wqkv", "Wqkvb"), ("Wout", "Woutb"),
                        ("W1", "W1b"), ("W2", "W2b")]:
            GP.dma_start(out=d[dst], in_=d[s_]); GP.inc(self.sCAST2, 16)
        self.castdone = self.sCAST2.n
        GP.dma_start(out=self.out, in_=d["xT"]); GP.inc(self.sDS, 16)

        V.memset(self.g9[:, 0:2048], 0.0); zv = V.inc(self.sVEC)
        SY.wait(self.sVEC, zv)
        SY.dma_start(out=d["ag2in"][ZROW:AG2R, :], in_=self.g9[:, 0:2048])
        SY.inc(self.sD, 16)

        self.early = early
        for e in ENGINES:
            P.eng(e).wait(self.sD, self.sD.n)
            P.eng(e).wait(self.sDG, self.sDG.n)
            P.eng(e).wait(self.sVEC, self.sVEC.n)
        self.plans.append(("p0", P))

    def ln_chunk(self, P, c, src=None):
        T, V, A, SY = P.eng("tensor"), P.eng("vector"), P.eng("scalar"), P.eng("sync")
        cw = slice(c * QC, (c + 1) * QC)
        r_m = self.g12[0:1, 0:512]
        r_v = self.g12[0:1, 512:1024]
        r_t = self.g12[0:1, 1024:1536]
        rb_m = self.g13[0:1, 0:512]
        rb_h = self.g13[0:1, 512:1024]
        rb_l = self.g13[0:1, 1024:1536]

        SY.wait(self.sDG, self.sDG.n)
        if src is None:
            src = self.out_r
            SY.wait(self.sDS, 16)
        SY.dma_start(out=self.g1.ap().rearrange("p (k q) -> p k q", k=NT),
                     in_=src[:, :, cw])
        ld = SY.inc(self.sD, 16)

        V.wait(self.sD, ld)
        sq_ms = []
        tq = [0] * NT
        for k in range(NT):
            V.tensor_copy(self.g3[:, k * QC:(k + 1) * QC],
                          self.g1[:, k * QC:(k + 1) * QC])
            V.inc(self.sVEC)
            if k >= 2:
                V.wait(self.sPE, tq[k - 2])
            V.tensor_tensor(self.g9[:, (k % 2) * QC:(k % 2) * QC + QC],
                            self.g3[:, k * QC:(k + 1) * QC],
                            self.g3[:, k * QC:(k + 1) * QC], ALU.mult)
            sq_ms.append(V.inc(self.sVEC))
            T.wait(self.sVEC, sq_ms[k])
            T.matmul(self.P7[0:1, :], self.c_ones[:],
                     self.g9[:, (k % 2) * QC:(k % 2) * QC + QC],
                     start=(k == 0), stop=(k == NT - 1))
            tq[k] = T.inc(self.sPE)
        mq = tq[NT - 1]
        A.wait(self.sPE, mq)
        A.activation(r_t, self.P7[0:1, :], AF.Copy, scale=1.0 / DM)
        aq = A.inc(self.sACT)
        for k in range(NT):
            if k == 0:
                T.wait(self.sACT, aq)
            T.matmul(self.P7[0:1, :], self.c_ones[:], self.g3[:, k * QC:(k + 1) * QC],
                     start=(k == 0), stop=(k == NT - 1))
        mx = T.inc(self.sPE)
        A.wait(self.sPE, mx)
        A.activation(r_m, self.P7[0:1, :], AF.Copy, scale=1.0 / DM)
        am = A.inc(self.sACT)

        V.wait(self.sACT, am)
        V.tensor_tensor(r_v, r_m, r_m, ALU.mult); V.inc(self.sVEC)
        V.tensor_tensor(r_v, r_t, r_v, ALU.subtract); vv = V.inc(self.sVEC)
        A.wait(self.sVEC, vv)
        A.activation(r_t, r_v, AF.Ln, bias=self.c_constp[0:1, 0:1])
        A.inc(self.sACT)
        A.activation(r_v, r_t, AF.Exp, scale=-0.5)
        ar = A.inc(self.sACT)
        V.wait(self.sACT, ar)
        V.tensor_copy(rb_h, r_v); V.inc(self.sVEC)
        V.tensor_tensor(r_t, r_v, rb_h, ALU.subtract); V.inc(self.sVEC)
        V.tensor_copy(rb_l, r_t); V.inc(self.sVEC)
        V.tensor_copy(rb_m, r_m); vr = V.inc(self.sVEC)

        T.wait(self.sVEC, vr)
        T.matmul(self.P6[0][:], self.c_onesr[:], rb_h, start=True, stop=False)
        T.matmul(self.P6[0][:], self.c_onesr[:], rb_l, start=False, stop=True)
        T.matmul(self.P6[1][:], self.c_onesr[:], rb_m, start=True, stop=True)
        tb = T.inc(self.sPE)
        V.wait(self.sPE, tb)
        V.tensor_copy(self.g11[:, 0:512], self.P6[0][:]); V.inc(self.sVEC)
        V.tensor_copy(self.g11[:, 512:1024], self.P6[1][:]); V.inc(self.sVEC)
        xl = []
        for k in range(NT):
            V.tensor_tensor(self.g9[:, (k % 2) * QC:(k % 2) * QC + QC],
                            self.g3[:, k * QC:(k + 1) * QC],
                            self.g11[:, 512:1024], ALU.subtract)
            V.inc(self.sVEC)
            V.tensor_tensor(self.g4[:, k * QC:(k + 1) * QC],
                            self.g9[:, (k % 2) * QC:(k % 2) * QC + QC],
                            self.g11[:, 0:512], ALU.mult)
            xl.append(V.inc(self.sVEC))
        return xl

    def attn_a(self):
        d = self.d
        for c in range(NQC):
            P = Plan(self)
            T, V, A = P.eng("tensor"), P.eng("vector"), P.eng("scalar")
            SY, GP = P.eng("sync"), P.eng("gpsimd")
            cw = slice(c * QC, (c + 1) * QC)
            csin = slice(TLOC + c * QC, TLOC + (c + 1) * QC)

            xl = self.ln_chunk(
                P, c, src=self.d["xT"].rearrange("(k p) q -> p k q", p=128))
            xlast = xl[NT - 1]
            if c == 0:
                SY.wait(self.sCAST, self.early)

            wsrc = [("WqTb", h, self.c_bq, h) for h in range(NH)] + \
                   [("WkTb", h, self.c_bk, h) for h in range(NKV)]
            pm_l = [0] * 24
            ev_l = [0] * 24
            rope_l = [0] * 24
            sq_l = [0] * 24
            pmm_l = [0] * 24
            pev_l = [0] * 24
            rm_l = [0] * 24
            wld = [0] * 24

            def s1(i):
                wn, h, btile, bcol = wsrc[i]
                if i >= 2:
                    SY.wait(self.sPE, pm_l[i - 2])
                buf = self.g7[:, (i % 2) * 2048:(i % 2) * 2048 + 2048]
                SY.dma_start(
                    out=buf.rearrange("p (k f) -> p k f", k=NT),
                    in_=d[wn].rearrange("(k p) f -> p k f", p=128)[:, :, h * 128:(h + 1) * 128])
                wld[i] = (self.sW[i % 2], SY.inc(self.sW[i % 2], 16))
                bank = self.P6[i % 2]
                T.wait(*wld[i])
                if i == 0:
                    T.wait(self.sVEC, xlast)
                if i >= 2:
                    T.wait(self.sACT, ev_l[i - 2])
                for k in range(NT):
                    T.matmul(bank[:], buf[:, k * 128:(k + 1) * 128],
                             self.g4[:, k * QC:(k + 1) * QC],
                             start=(k == 0), stop=False)
                T.matmul(bank[:], btile[0:1, bcol * 128:(bcol + 1) * 128],
                         self.c_o512[:], start=False, stop=True)
                pm_l[i] = T.inc(self.sPE)
                qb = self.g9[:, (i % 2) * QC:(i % 2) * QC + QC]
                A.wait(self.sPE, pm_l[i])
                if i >= 2:
                    A.wait(self.sVEC, rope_l[i - 2])
                A.activation(qb, bank[:], AF.Copy)
                ev_l[i] = A.inc(self.sACT)

            def s2(i):
                rbank = self.P6[2 + i % 2]
                qb = self.g9[:, (i % 2) * QC:(i % 2) * QC + QC]
                T.wait(self.sACT, ev_l[i])
                if i >= 2:
                    T.wait(self.sVEC, rope_l[i - 2])
                T.matmul(rbank[:], self.c_prot[:], qb, start=True, stop=True)
                rm_l[i] = T.inc(self.sPE)
                ro = self.g5[:, (i % 2) * QC:(i % 2) * QC + QC]
                t1 = self.g9[:, 1024 + (i % 2) * QC:1024 + (i % 2) * QC + QC]
                V.wait(self.sPE, rm_l[i])
                if i >= 2:
                    V.wait(self.sPE, pmm_l[i - 2])
                V.tensor_tensor(t1, qb, self.c_csb[:, cw], ALU.mult)
                V.inc(self.sVEC)
                V.tensor_tensor(ro, rbank[:], self.c_csb[:, csin], ALU.mult)
                V.inc(self.sVEC)
                V.tensor_tensor(ro, ro, t1, ALU.add)
                rope_l[i] = V.inc(self.sVEC)
                sqb = self.g15[:, (i % 2) * QC:(i % 2) * QC + QC]
                V.tensor_tensor(sqb, ro, ro, ALU.mult)
                sq_l[i] = V.inc(self.sVEC)

            def s3(i):
                ro = self.g5[:, (i % 2) * QC:(i % 2) * QC + QC]
                sqb = self.g15[:, (i % 2) * QC:(i % 2) * QC + QC]
                T.wait(self.sVEC, sq_l[i])
                if i >= 1:
                    T.wait(self.sACT, pev_l[i - 1])
                for qs in range(4):
                    T.matmul(self.P6[4][:, qs * 64:(qs + 1) * 64],
                             ro[:, qs * 128:(qs + 1) * 128], self.c_rt[:],
                             start=True, stop=True)
                for qs in range(4):
                    T.matmul(self.P6[5][:, (qs * 24 + i):(qs * 24 + i) + 1],
                             sqb[:, qs * 128:(qs + 1) * 128], self.c_ones[:],
                             start=True, stop=True)
                pmm_l[i] = T.inc(self.sPE)
                A.wait(self.sPE, pmm_l[i])
                for qs in range(4):
                    A.activation(self.g1[:, qs * 1536 + i * 64:qs * 1536 + i * 64 + 64],
                                 self.P6[4][:, qs * 64:(qs + 1) * 64], AF.Copy)
                pev_l[i] = A.inc(self.sACT)

            for step in range(26):
                if step < 24:
                    s1(step)
                if 0 <= step - 1 < 24:
                    s2(step - 1)
                if 0 <= step - 2 < 24:
                    s3(step - 2)
            ptm_last = pev_l[23]
            rope_done = [rope_l[22], rope_l[23]]
            pmm_done = [pmm_l[22], pmm_l[23]]
            vth = [0] * (2 * NT)
            vi = 0
            vp_ev = 0
            for nch in range(2):
                for k in range(NT):
                    idx = nch * NT + k
                    if idx >= 4:
                        SY.wait(self.sPE, vth[idx - 4])
                    else:
                        SY.wait(self.sPE, pmm_done[0])
                        SY.wait(self.sPE, pmm_done[1])
                    SY.dma_start(out=self.g7[:, (idx % 4) * 512:(idx % 4) * 512 + 512],
                                 in_=d["WvTb"][k * 128:(k + 1) * 128,
                                               nch * 512:(nch + 1) * 512])
                    vld = SY.inc(self.sW[idx % 4], 16)
                    T.wait(self.sW[idx % 4], vld)
                    if idx == 0:
                        T.wait(self.sACT, ev_l[23])
                        T.wait(self.sVEC, rope_done[0])
                        T.wait(self.sVEC, rope_done[1])
                    if k == 0 and nch == 1:
                        T.wait(self.sVEC, vp_ev)
                    for qs in range(4):
                        T.matmul(self.P6[qs][:],
                                 self.g4[:, k * QC + qs * 128:k * QC + qs * 128 + 128],
                                 self.g7[:, (idx % 4) * 512:(idx % 4) * 512 + 512],
                                 start=(k == 0), stop=False)
                    vth[idx] = T.inc(self.sPE)
                for qs in range(4):
                    T.matmul(self.P6[qs][:], self.c_onesr[:],
                             self.c_bv[0:1, nch * 512:(nch + 1) * 512],
                             start=False, stop=True)
                vb = T.inc(self.sPE)
                V.wait(self.sPE, vb)
                for qs in range(4):
                    V.tensor_copy(self.g6[:, qs * 1024 + nch * 512:qs * 1024 + nch * 512 + 512],
                                  self.P6[qs][:])
                vp_ev = V.inc(self.sVEC)

            vmm_last = self.sPE.n
            rnt = self.g14[:, 0:96]
            V.wait(self.sPE, pmm_done[0])
            V.wait(self.sPE, pmm_done[1])
            V.tensor_copy(rnt, self.P6[5][:, 0:96]); rc = V.inc(self.sVEC)
            A.wait(self.sVEC, rc)
            A.activation(rnt, rnt, AF.Sqrt)
            sq_a = A.inc(self.sACT)
            V.wait(self.sACT, sq_a)
            V.tensor_scalar(rnt, rnt, 1e-6, TWOPI, ALU.max, ALU.mult)
            V.inc(self.sVEC)
            V.reciprocal(rnt, rnt)
            rn_m = V.inc(self.sVEC)

            wo_ready = []
            for qs in range(4):
                ptm3 = self.g1[:, qs * 1536:(qs + 1) * 1536].rearrange(
                    "p (h f) -> p h f", f=FE)
                rnv = self.g14[:, qs * 24:qs * 24 + 24].unsqueeze(2).broadcast_to(
                    [128, 24, FE])
                u = self.g2[:, 4096:5632]
                u3 = u.rearrange("p (h f) -> p h f", f=FE)
                w = self.g2[:, 5632:7168]
                w3 = w.rearrange("p (h f) -> p h f", f=FE)
                V.wait(self.sACT, ptm_last)
                V.wait(self.sVEC, rn_m)
                V.wait(self.sDG, self.sDG.n)
                V.tensor_tensor(u3, ptm3, rnv, ALU.mult); V.inc(self.sVEC)
                V.tensor_scalar(w, u, MAGIC, -MAGIC, ALU.add, ALU.add)
                V.inc(self.sVEC)
                V.tensor_tensor(w, u, w, ALU.subtract)
                wm1 = V.inc(self.sVEC)
                qp3 = self.g8.ap().rearrange("p (h f) -> p h f", f=128)
                A.wait(self.sVEC, wm1)
                A.activation(qp3[:, :, 0:FE], w3, AF.Sin, scale=TWOPI)
                as1 = A.inc(self.sACT)
                V.wait(self.sACT, as1)
                V.tensor_scalar(u, u, 0.25, None, ALU.add)
                V.inc(self.sVEC)
                V.tensor_scalar(w, u, MAGIC, -MAGIC, ALU.add, ALU.add)
                V.inc(self.sVEC)
                V.tensor_tensor(w, u, w, ALU.subtract)
                wm2 = V.inc(self.sVEC)
                A.wait(self.sVEC, wm2)
                A.activation(qp3[:, :, FE:128], w3, AF.Sin, scale=TWOPI)
                as2 = A.inc(self.sACT)

                Acols = self.g14[:, 128:256]
                V.wait(self.sACT, as2)
                if qs > 0:
                    V.wait(self.sW[2], wo_ready[-1])
                g15_3 = self.g15.ap().rearrange("p (h f) -> p h f", f=128)
                for hp in range(8):
                    kp = self.g8[:, (NH + hp) * 128:(NH + hp + 1) * 128]
                    kv = kp.unsqueeze(1).broadcast_to([128, NH, 128])
                    qwin = self.g8[:, 0:2048].rearrange("p (h f) -> p h f", f=128)
                    V.tensor_tensor(g15_3, qwin, kv, ALU.mult)
                    V.inc(self.sVEC)
                    V.tensor_reduce(Acols[:, hp * 16:(hp + 1) * 16], g15_3,
                                    axis=AX.X, op=ALU.add)
                    V.inc(self.sVEC)
                Ab = self.g9[:, 0:128]
                V.tensor_scalar(Ab, Acols, 2.0 / 128.0, None, ALU.mult)
                V.inc(self.sVEC)
                mix = self.g2[:, 0:2048]
                mix3 = mix.rearrange("p (h f) -> p h f", f=128)
                for hp in range(8):
                    Av = Ab[:, hp * 16:(hp + 1) * 16].unsqueeze(2).broadcast_to(
                        [128, NH, 128])
                    vv3 = self.g6[:, qs * 1024 + hp * 128:qs * 1024 + (hp + 1) * 128]\
                        .unsqueeze(1).broadcast_to([128, NH, 128])
                    if hp == 0:
                        V.tensor_tensor(mix3, Av, vv3, ALU.mult)
                        V.inc(self.sVEC)
                    else:
                        V.tensor_tensor(g15_3, Av, vv3, ALU.mult)
                        V.inc(self.sVEC)
                        V.tensor_tensor(mix, mix, self.g15[:], ALU.add)
                        V.inc(self.sVEC)
                mixb = self.g15[:]
                V.tensor_copy(mixb, mix)
                mxm = V.inc(self.sVEC)
                SY.wait(self.sVEC, mxm)
                SY.wait(self.sW[2], self.sW[2].n)
                SY.dma_start(
                    out=d["mixh"].rearrange("(h s) d -> s h d", h=NH)[
                        c * QC + qs * 128: c * QC + qs * 128 + 128, :, :],
                    in_=mixb.rearrange("p (h d) -> p h d", h=NH))
                mw = SY.inc(self.sW[2], 16)
                wo_ready.append(mw)
            P.barrier()
            self.plans.append((f"attna{c}", P))

    def attna_fin(self):
        import os
        kfin = int(os.environ.get("KFIN", "3"))
        d = self.d
        P = Plan(self)
        T, A, V = P.eng("tensor"), P.eng("scalar"), P.eng("vector")
        SY, GP = P.eng("sync"), P.eng("gpsimd")
        groups = [[2 * i, 2 * i + 1] for i in range(4)]
        GP.collective_compute("AllGather", ALU.bypass, replica_groups=groups,
                              ins=[d["mixh"].opt()], outs=[d["mixhg"].opt()])
        cc = GP.inc(self.sCC, 1)
        if kfin < 1:
            P.barrier()
            self.plans.append(("attna_fin", P))
            return
        mixview = d["mixhg"].rearrange("(n s) d -> n (s d)", s=16)
        tev = 0
        trdone = 0
        for c in range(NQC):
            cw = slice(c * QC, (c + 1) * QC)
            for tsub in range(4):
                j = c * 4 + tsub
                GP.wait(self.sCC, cc)
                if c > 0 or tsub > 0:
                    GP.wait(self.sPE, trdone)
                GP.indirect_dma_start(
                    out=self.g3[:, tsub * 2048:(tsub + 1) * 2048],
                    out_offset=None, in_=mixview,
                    in_offset=IndirectOffsetOnAxis(
                        self.c_idxscr[:, j:j + 1], axis=0),
                    oob_is_err=False)
                gm = GP.inc(self.sG[tsub], 16)
                for grp in range(2):
                    T.wait(self.sG[tsub], gm)
                    if not (c == 0 and tsub == 0 and grp == 0):
                        T.wait(self.sVEC, tev)
                    for k in range(grp * 8, grp * 8 + 8):
                        T.transpose(self.PT[:, (k % 8) * 128:(k % 8) * 128 + 128],
                                    self.g3[:, tsub * 2048 + k * 128:
                                            tsub * 2048 + (k + 1) * 128],
                                    self.c_ident[:])
                    tm = T.inc(self.sPE)
                    V.wait(self.sPE, tm)
                    for k in range(grp * 8, grp * 8 + 8):
                        V.tensor_copy(
                            self.g5[:, k * QC + tsub * 128:k * QC + tsub * 128 + 128],
                            self.PT[:, (k % 8) * 128:(k % 8) * 128 + 128])
                    tev = V.inc(self.sVEC)
                trdone = self.sPE.n
            if kfin < 2:
                continue
            wpm = [0] * NH
            aev_l = [0] * NH
            for ft in range(NH):
                if ft >= 2:
                    SY.wait(self.sPE, wpm[ft - 2])
                buf = self.g7[:, (ft % 2) * 2048:(ft % 2) * 2048 + 2048]
                if ft == 0:
                    SY.wait(self.sCAST2, self.castdone)
                SY.dma_start(
                    out=buf.rearrange("p (k f) -> p k f", k=NT),
                    in_=d["WoTb"].rearrange("(k p) f -> p k f", p=128)[:, :, ft * 128:(ft + 1) * 128])
                wl = SY.inc(self.sW[ft % 2], 16)
                bank = self.P6[ft % 2]
                T.wait(self.sW[ft % 2], wl)
                if ft == 0:
                    T.wait(self.sVEC, tev)
                if ft >= 2:
                    T.wait(self.sACT, aev_l[ft - 2])
                for k in range(NT):
                    T.matmul(bank[:], buf[:, k * 128:(k + 1) * 128],
                             self.g5[:, k * QC:(k + 1) * QC],
                             start=(k == 0), stop=False)
                T.matmul(bank[:], self.c_bo[0:1, ft * 128:(ft + 1) * 128],
                         self.c_o512[:], start=False, stop=True)
                wpm[ft] = T.inc(self.sPE)
                A.wait(self.sPE, wpm[ft])
                if ft == 0:
                    A.wait(self.sDG, self.sDG.n)
                A.activation(self.g2[:, ft * QC:(ft + 1) * QC], bank[:], AF.Copy)
                aev_l[ft] = A.inc(self.sACT)
            GP.wait(self.sACT, aev_l[NH - 1])
            GP.wait(self.sDS, 16)
            GP.dma_start(out=self.out_r[:, :, cw],
                         in_=self.g2[:, 0:8192].rearrange("p (k q) -> p k q", k=NT),
                         accum_op=ALU.add)
            GP.inc(self.sDG, 16)
            P.barrier()
        self.plans.append(("attna_fin", P))

    def ffn(self, tag):
        d = self.d
        for c in range(NQC):
            P = Plan(self)
            T, V, A = P.eng("tensor"), P.eng("vector"), P.eng("scalar")
            SY, GP = P.eng("sync"), P.eng("gpsimd")
            xl = self.ln_chunk(P, c)
            xlast = xl[NT - 1]
            agrows = 0
            tev = 0
            for qs in range(4):
                for grp in range(2):
                    if qs == 0 and grp == 0:
                        T.wait(self.sVEC, xlast)
                    else:
                        T.wait(self.sVEC, tev)
                    for k in range(grp * 8, grp * 8 + 8):
                        T.transpose(self.PT[:, (k % 8) * 128:(k % 8) * 128 + 128],
                                    self.g4[:, k * QC + qs * 128:k * QC + qs * 128 + 128],
                                    self.c_ident[:])
                    tm = T.inc(self.sPE)
                    V.wait(self.sPE, tm)
                    for k in range(grp * 8, grp * 8 + 8):
                        V.tensor_copy(self.g3[:, qs * 2048 + k * 128:qs * 2048 + (k + 1) * 128],
                                      self.PT[:, (k % 8) * 128:(k % 8) * 128 + 128])
                    tev = V.inc(self.sVEC)
                SY.wait(self.sVEC, tev)
                SY.wait(self.sW[3], self.sW[3].n)
                SY.dma_start(out=d["ag1in"][c * QC + qs * 128:c * QC + qs * 128 + 128, :],
                             in_=self.g3[:, qs * 2048:(qs + 1) * 2048])
                agrows = SY.inc(self.sW[3], 16)
            gl = 0
            for qs in range(4):
                for k in range(NT):
                    if qs == 0 and k == 0:
                        T.wait(self.sACT, self.sACT.n)
                    T.matmul(self.P6[2][:, qs * 8:(qs + 1) * 8],
                             self.g4[:, k * QC + qs * 128:k * QC + qs * 128 + 128],
                             self.c_gT[:, k * 8:(k + 1) * 8],
                             start=(k == 0), stop=(k == NT - 1))
            gl = T.inc(self.sPE)
            A.wait(self.sPE, gl)
            A.activation(self.g10[:, c * 32:(c + 1) * 32], self.P6[2][:, 0:32],
                         AF.Copy)
            A.inc(self.sACT)
            GP.wait(self.sW[3], agrows)
            GP.collective_compute(
                "AllGather", ALU.bypass, replica_groups=[list(range(NCORES))],
                ins=[d["ag1in"][c * QC:(c + 1) * QC, :].opt()],
                outs=[d["ag1out"][c * NCORES * QC:(c + 1) * NCORES * QC, :].opt()])
            GP.inc(self.sCC, 1)
            if c == NQC - 1:
                self.cag1 = self.sCC.n
            P.barrier()
            self.plans.append((f"ffnln{tag}_{c}", P))

        P = Plan(self)
        T, V, A = P.eng("tensor"), P.eng("vector"), P.eng("scalar")
        SY, GP = P.eng("sync"), P.eng("gpsimd")
        V.memset(self.g11[:, 0:168], 0.0)
        zv = V.inc(self.sVEC)
        SY.wait(self.sVEC, zv)
        SY.dma_start(out=d["fragi"].rearrange("(p f) one -> p (f one)", f=168),
                     in_=self.g11[:, 0:168])
        SY.inc(self.sD, 16)
        SY.dma_start(out=d["fragw"].rearrange("(p f) one -> p (f one)", f=168),
                     in_=self.g11[:, 0:168])
        zd = SY.inc(self.sD, 16)

        V.wait(self.sD, zd)
        cnt_ms = 0
        for j in range(NT):
            l = self.g10[:, j * 8:(j + 1) * 8]
            t0 = self.g11[:, 0:8]
            t1 = self.g11[:, 8:16]
            t2 = self.g11[:, 16:24]
            mcol = self.g11[:, 24:25]
            scol = self.g11[:, 25:26]
            V.reduce_max(mcol, l, axis=AX.X)
            V.tensor_scalar(mcol, mcol, -1.0, None, ALU.mult)
            vm = V.inc(self.sVEC)
            A.wait(self.sVEC, vm)
            A.activation(t0, l, AF.Exp, bias=mcol)
            ae = A.inc(self.sACT)
            V.wait(self.sACT, ae)
            V.reduce_sum(scol, t0, axis=AX.X)
            V.reciprocal(self.g14[:, j:j + 1], scol)
            V.tensor_scalar(t0, l, mcol, None, ALU.add)
            V.tensor_scalar(t0, t0, -0.0, None, ALU.is_ge)
            V.tensor_copy(t1, t0)
            V.tensor_tensor(t1[:, 1:8], t0[:, 1:8], t0[:, 0:7], ALU.add)
            V.tensor_copy(t2, t1)
            V.tensor_tensor(t2[:, 2:8], t1[:, 2:8], t1[:, 0:6], ALU.add)
            V.tensor_copy(t1, t2)
            V.tensor_tensor(t1[:, 4:8], t2[:, 4:8], t2[:, 0:4], ALU.add)
            V.tensor_scalar(t1, t1, 1.0, None, ALU.is_equal)
            V.tensor_tensor(t0, t0, t1, ALU.mult)
            V.tensor_copy(self.g8[:, j * 8:(j + 1) * 8], t0)
            fb = V.inc(self.sVEC)
            T.wait(self.sVEC, fb)
            T.matmul(self.P6[5][0:8, j:j + 1], self.g8[:, j * 8:(j + 1) * 8],
                     self.c_ones[:], start=True, stop=True)
            cnt_ms = T.inc(self.sPE)
        A.wait(self.sPE, cnt_ms)
        A.activation(self.g11[0:8, 512:528], self.P6[5][0:8, 0:16], AF.Copy)
        cev = A.inc(self.sACT)
        SY.wait(self.sACT, cev)
        SY.dma_start(out=d["cntin"].rearrange("j e -> e j"),
                     in_=self.g11[0:8, 512:528],
                     allow_slow_non_contiguous=True)
        cnd = SY.inc(self.sD, 16)
        GP.wait(self.sD, cnd)
        GP.collective_compute("AllGather", ALU.bypass,
                              replica_groups=[list(range(NCORES))],
                              ins=[d["cntin"].opt()], outs=[d["cntout"].opt()])
        GP.inc(self.sCC, 1)
        ccc = self.sCC.n
        cag1 = self.cag1

        SY.wait(self.sCC, ccc)
        SY.dma_start(out=self.g11[:, 768:776], in_=d["cntout"])
        cl = SY.inc(self.sD, 16)
        V.wait(self.sD, cl)
        V.tensor_copy(self.g13[:, 0:8], self.g11[:, 768:776])
        cbf = V.inc(self.sVEC)

        rk_ev = 0
        rkread = 0
        for j in range(NT):
            bank = self.P6[j % 2]
            T.wait(self.sVEC, cbf)
            if j >= 2:
                T.wait(self.sACT, rk_ev)
            T.matmul(bank[:, 0:8], self.c_tri[:], self.g8[:, j * 8:(j + 1) * 8],
                     start=True, stop=False)
            T.matmul(bank[:, 0:8], self.c_trimyr[:, j * 128:(j + 1) * 128],
                     self.g13[:, 0:8], start=False, stop=False)
            T.matmul(bank[:, 0:8], self.c_onesr[:], self.c_nego[0:1, 0:8],
                     start=False, stop=True)
            rkm = T.inc(self.sPE)
            A.wait(self.sPE, rkm)
            if rkread:
                A.wait(self.sVEC, rkread)
            A.activation(self.g11[:, 32:40], bank[:, 0:8], AF.Copy)
            rk_ev = A.inc(self.sACT)
            rk = self.g11[:, 32:40]
            pr = self.g11[:, 40:48]
            rsel = self.g11[:, 48:49]
            esel = self.g11[:, 49:50]
            keep = self.g11[:, 50:51]
            wtok = self.g11[:, 51:52]
            tmp = self.g11[:, 52:53]
            tmp2 = self.g11[:, 53:54]
            tmp3 = self.g11[:, 54:55]
            V.wait(self.sACT, rk_ev)
            V.tensor_tensor(pr, self.g8[:, j * 8:(j + 1) * 8], rk, ALU.mult)
            V.reduce_sum(rsel, pr, axis=AX.X)
            V.tensor_tensor(pr, self.g8[:, j * 8:(j + 1) * 8], self.c_iota[:],
                            ALU.mult)
            V.reduce_sum(esel, pr, axis=AX.X)
            rkread = V.inc(self.sVEC)
            V.tensor_scalar(keep, rsel, float(CAP) - 0.5, None, ALU.is_lt)
            V.tensor_tensor(wtok, self.g14[:, j:j + 1], keep, ALU.mult)
            V.tensor_scalar(tmp, esel, float(CAPP), None, ALU.mult)
            V.tensor_tensor(tmp2, tmp, rsel, ALU.add)
            V.tensor_scalar(tmp3, tmp, 2560.0, None, ALU.add)
            V.tensor_tensor(tmp, tmp2, tmp3, ALU.subtract)
            V.tensor_scalar(tmp, tmp, keep, None, ALU.mult)
            V.tensor_tensor(tmp, tmp, tmp3, ALU.add)
            V.tensor_copy(self.c_sci[:, j % 2:j % 2 + 1], tmp)
            sci = V.inc(self.sVEC)
            GP.wait(self.sVEC, sci)
            GP.indirect_dma_start(
                out=d["fragi"],
                out_offset=IndirectOffsetOnAxis(self.c_sci[:, j % 2:j % 2 + 1], axis=0),
                in_=self.c_gidx[:, j:j + 1], in_offset=None, oob_is_err=False)
            GP.inc(self.sSC, 16)
            GP.indirect_dma_start(
                out=d["fragw"],
                out_offset=IndirectOffsetOnAxis(self.c_sci[:, j % 2:j % 2 + 1], axis=0),
                in_=wtok, in_offset=None, oob_is_err=False)
            sc2 = GP.inc(self.sSC, 16)
            V.wait(self.sSC, sc2)
            V.tensor_scalar(tmp, rsel, -10.0, 1.0 / 21.0, ALU.add, ALU.mult)
            V.tensor_scalar(tmp, tmp, MAGIC, -MAGIC, ALU.add, ALU.add)
            V.scalar_tensor_tensor(tmp2, tmp, -21.0, rsel, ALU.mult, ALU.add)
            V.tensor_scalar(tmp3, tmp2, 10.5, None, ALU.is_gt)
            V.scalar_tensor_tensor(tmp, tmp3, -1408.0, tmp, ALU.mult, ALU.add)
            V.scalar_tensor_tensor(tmp, tmp2, 128.0, tmp, ALU.mult, ALU.add)
            V.scalar_tensor_tensor(tmp, tmp3, 11264.0 + 1408.0 * 0, tmp,
                                   ALU.mult, ALU.add)
            V.scalar_tensor_tensor(tmp, esel, 1408.0, tmp, ALU.mult, ALU.add)
            V.tensor_tensor(tmp2, tmp, self.c_constp[:, 3:4], ALU.subtract)
            V.tensor_scalar(tmp2, tmp2, keep, None, ALU.mult)
            V.tensor_tensor(tmp2, tmp2, self.c_constp[:, 3:4], ALU.add)
            V.tensor_copy(self.c_cri[:, j:j + 1], tmp2)
            V.inc(self.sVEC)
        GP.wait(self.sVEC, self.sVEC.n)
        GP.wait(self.sSC, self.sSC.n)
        GP.collective_compute("ReduceScatter", ALU.add,
                              replica_groups=[list(range(NCORES))],
                              ins=[d["fragi"].opt()], outs=[d["fragio"].opt()])
        GP.inc(self.sCC, 1)
        GP.collective_compute("ReduceScatter", ALU.add,
                              replica_groups=[list(range(NCORES))],
                              ins=[d["fragw"].opt()], outs=[d["fragwo"].opt()])
        GP.inc(self.sCC, 1)
        crs = self.sCC.n
        P.barrier()
        self.plans.append((f"route{tag}", P))

        P = Plan(self)
        T, V, A = P.eng("tensor"), P.eng("vector"), P.eng("scalar")
        SY, GP = P.eng("sync"), P.eng("gpsimd")
        SY.wait(self.sCC, crs)
        SY.dma_start(out=self.g11[:, 0:21],
                     in_=d["fragio"].rearrange("(p f) one -> p (f one)", f=21))
        SY.inc(self.sD, 16)
        SY.dma_start(out=self.g11[:, 32:53],
                     in_=d["fragwo"].rearrange("(p f) one -> p (f one)", f=21))
        sl = SY.inc(self.sD, 16)
        V.wait(self.sD, sl)
        V.tensor_scalar(self.g11[:, 0:21], self.g11[:, 0:21],
                        float(NCORES * TLOC - 1), None, ALU.min)
        V.tensor_copy(self.c_sci[:, 2:23], self.g11[:, 0:21])
        siv = V.inc(self.sVEC)

        SY.wait(self.sCAST2, self.castdone)
        fgroups = [list(range(0, 4)), list(range(4, 8)), list(range(8, 11)),
                   list(range(11, 15)), list(range(15, 19)), list(range(19, 21))]
        bankfree = [0, 0, 0, 0]
        slotfree = [0, 0, 0, 0]
        tr_ms = 0
        otm = 0
        g3free = 0
        ag2flags = []
        for gi, fg in enumerate(fgroups):
            nw = len(fg) * 128
            gath_ms = [0, 0, 0, 0]
            for si, f in enumerate(fg):
                GP.wait(self.sVEC, siv)
                GP.wait(self.sCC, cag1)
                if gi > 0:
                    GP.wait(self.sPE, g3free)
                GP.indirect_dma_start(
                    out=self.g3[:, si * 2048:(si + 1) * 2048], out_offset=None,
                    in_=d["ag1out"],
                    in_offset=IndirectOffsetOnAxis(self.c_sci[:, 2 + f:3 + f], axis=0),
                    oob_is_err=False)
                gath_ms[si] = GP.inc(self.sG[si], 16)
            prev_t = otm
            for si in range(len(fg)):
                for grp in range(2):
                    T.wait(self.sG[si % 4], gath_ms[si])
                    if prev_t:
                        T.wait(self.sVEC, prev_t)
                    for k in range(grp * 8, grp * 8 + 8):
                        T.transpose(self.PT[:, (k % 8) * 128:(k % 8) * 128 + 128],
                                    self.g3[:, si * 2048 + k * 128:si * 2048 + (k + 1) * 128],
                                    self.c_ident[:])
                    tm = T.inc(self.sPE)
                    V.wait(self.sPE, tm)
                    for k in range(grp * 8, grp * 8 + 8):
                        V.tensor_copy(self.g5[:, k * nw + si * 128:k * nw + si * 128 + 128],
                                      self.PT[:, (k % 8) * 128:(k % 8) * 128 + 128])
                    prev_t = V.inc(self.sVEC)
            tr_ms = self.sPE.n
            xfm_ok = prev_t
            hev = [0] * 16
            for it in range(16):
                SY.wait(self.sPE, slotfree[it % 2])
                buf = self.g7[:, (it % 2) * 2048:(it % 2) * 2048 + 2048]
                SY.dma_start(
                    out=buf.rearrange("p (k f) -> p k f", k=NT),
                    in_=d["W1b"].rearrange("(k p) f -> p k f", p=128)[:, :, it * 128:(it + 1) * 128])
                wl = SY.inc(self.sW[it % 2], 16)
                bank = self.P6[it % 2]
                T.wait(self.sW[it % 2], wl)
                if it == 0:
                    T.wait(self.sVEC, xfm_ok)
                T.wait(self.sACT, bankfree[it % 2])
                for k in range(NT):
                    T.matmul(bank[:, 0:nw],
                             buf.rearrange("p (k f) -> p k f", k=NT)[:, k, :],
                             self.g5[:, k * nw:(k + 1) * nw],
                             start=(k == 0), stop=(k == NT - 1))
                hm = T.inc(self.sPE)
                slotfree[it % 2] = hm
                A.wait(self.sPE, hm)
                A.activation(self.g4[:, it * nw:(it + 1) * nw], bank[:, 0:nw],
                             AF.Gelu)
                hev[it] = A.inc(self.sACT)
                bankfree[it % 2] = hev[it]
            oev = [0] * 16
            for ht in range(16):
                SY.wait(self.sPE, slotfree[2 + ht % 2])
                buf = self.g6[:, (ht % 2) * 2048:(ht % 2) * 2048 + 2048]
                SY.dma_start(
                    out=buf.rearrange("p (k f) -> p k f", k=NT),
                    in_=d["W2b"].rearrange("(k p) f -> p k f", p=128)[:, :, ht * 128:(ht + 1) * 128])
                wl = SY.inc(self.sW[2 + ht % 2], 16)
                bank = self.P6[2 + ht % 2]
                T.wait(self.sW[2 + ht % 2], wl)
                if ht == 0:
                    T.wait(self.sACT, hev[15])
                    T.wait(self.sPE, tr_ms)
                T.wait(self.sACT, bankfree[2 + ht % 2])
                for k in range(NT):
                    T.matmul(bank[:, 0:nw],
                             buf.rearrange("p (k f) -> p k f", k=NT)[:, k, :],
                             self.g4[:, k * nw:(k + 1) * nw],
                             start=(k == 0), stop=(k == NT - 1))
                om = T.inc(self.sPE)
                slotfree[2 + ht % 2] = om
                A.wait(self.sPE, om)
                A.activation(self.g3[:, ht * nw:(ht + 1) * nw], bank[:, 0:nw],
                             AF.Copy)
                oev[ht] = A.inc(self.sACT)
                bankfree[2 + ht % 2] = oev[ht]
            for si, f in enumerate(fg):
                for grp in range(2):
                    T.wait(self.sACT, oev[15])
                    if otm:
                        T.wait(self.sVEC, otm)
                    for ht in range(grp * 8, grp * 8 + 8):
                        T.transpose(self.PT[:, (ht % 8) * 128:(ht % 8) * 128 + 128],
                                    self.g3[:, ht * nw + si * 128:ht * nw + si * 128 + 128],
                                    self.c_ident[:])
                    tm = T.inc(self.sPE)
                    V.wait(self.sPE, tm)
                    if grp == 0:
                        V.wait(self.sW[3], self.sW[3].n)
                    for ht in range(grp * 8, grp * 8 + 8):
                        V.tensor_copy(self.g15[:, ht * 128:(ht + 1) * 128],
                                      self.PT[:, (ht % 8) * 128:(ht % 8) * 128 + 128])
                    otm = V.inc(self.sVEC)
                V.tensor_scalar(self.g15[:], self.g15[:],
                                self.g11[:, 32 + f:33 + f], None, ALU.mult)
                sc = V.inc(self.sVEC)
                SY.wait(self.sVEC, sc)
                SY.dma_start(out=d["ag2in"][f * 128:(f + 1) * 128, :],
                             in_=self.g15[:])
                SY.inc(self.sW[3], 16)
            g3free = self.sPE.n
            if gi == 2:
                GP.wait(self.sW[3], self.sW[3].n)
                GP.collective_compute(
                    "AllGather", ALU.bypass, replica_groups=[list(range(NCORES))],
                    ins=[d["ag2in"][0:1408, :].opt()],
                    outs=[d["ag2out"][0:8 * 1408, :].opt()])
                GP.inc(self.sCC, 1)
                ag2flags.append(self.sCC.n)
            if gi == 5:
                GP.wait(self.sW[3], self.sW[3].n)
                GP.collective_compute(
                    "AllGather", ALU.bypass, replica_groups=[list(range(NCORES))],
                    ins=[d["ag2in"][1408:AG2R, :].opt()],
                    outs=[d["ag2out"][8 * 1408:2 * 8 * 1408, :].opt()])
                GP.inc(self.sCC, 1)
                ag2flags.append(self.sCC.n)
        P.barrier()
        self.plans.append((f"expert{tag}", P))

        P = Plan(self)
        T, V, A = P.eng("tensor"), P.eng("vector"), P.eng("scalar")
        SY, GP = P.eng("sync"), P.eng("gpsimd")
        gme = [0, 0, 0, 0]
        tev = 0
        trprev = 0
        for c in range(NQC):
            cw = slice(c * QC, (c + 1) * QC)
            for tsub in range(4):
                j = c * 4 + tsub
                GP.wait(self.sCC, ag2flags[1])
                if c > 0:
                    GP.wait(self.sPE, trprev)
                GP.indirect_dma_start(
                    out=self.g3[:, tsub * 2048:(tsub + 1) * 2048], out_offset=None,
                    in_=d["ag2out"],
                    in_offset=IndirectOffsetOnAxis(self.c_cri[:, j:j + 1], axis=0),
                    oob_is_err=False)
                gme[tsub] = GP.inc(self.sG[tsub], 16)
            for tsub in range(4):
                for grp in range(2):
                    T.wait(self.sG[tsub], gme[tsub])
                    if tev:
                        T.wait(self.sVEC, tev)
                    for k in range(grp * 8, grp * 8 + 8):
                        T.transpose(self.PT[:, (k % 8) * 128:(k % 8) * 128 + 128],
                                    self.g3[:, tsub * 2048 + k * 128:tsub * 2048 + (k + 1) * 128],
                                    self.c_ident[:])
                    tm = T.inc(self.sPE)
                    V.wait(self.sPE, tm)
                    for k in range(grp * 8, grp * 8 + 8):
                        V.tensor_copy(
                            self.g5[:, k * QC + tsub * 128:k * QC + tsub * 128 + 128],
                            self.PT[:, (k % 8) * 128:(k % 8) * 128 + 128])
                    tev = V.inc(self.sVEC)
            trprev = self.sPE.n
            V.wait(self.sDG, self.sDG.n)
            for k in range(NT):
                V.tensor_copy(self.g2[:, k * QC:(k + 1) * QC],
                              self.g5[:, k * QC:(k + 1) * QC])
            cv = V.inc(self.sVEC)
            GP.wait(self.sVEC, cv)
            GP.dma_start(out=self.out_r[:, :, cw],
                         in_=self.g2[:, 0:8192].rearrange("p (k q) -> p k q", k=NT),
                         accum_op=ALU.add)
            GP.inc(self.sDG, 16)
            P.barrier()
        self.plans.append((f"combine{tag}", P))

    def attn_b(self):
        d = self.d
        for c in range(NQC):
            P = Plan(self)
            T, V, A = P.eng("tensor"), P.eng("vector"), P.eng("scalar")
            SY, GP = P.eng("sync"), P.eng("gpsimd")
            cw = slice(c * QC, (c + 1) * QC)
            xl = self.ln_chunk(P, c)
            xlast = xl[NT - 1]
            if c == 0:
                SY.wait(self.sCAST2, self.castdone)
            wld = [0, 0]
            evms = [0, 0]
            consumed = [0, 0]
            for i in range(32):
                isk = i < 16
                h = i % 16
                fcol = (2048 + h * 128) if isk else (h * 128)
                if i >= 2:
                    SY.wait(self.sPE, wld[i % 2])
                buf = self.g7[:, (i % 2) * 2048:(i % 2) * 2048 + 2048]
                SY.dma_start(
                    out=buf.rearrange("p (k f) -> p k f", k=NT),
                    in_=d["Wqkvb"].rearrange("(k p) f -> p k f", p=128)[:, :, fcol:fcol + 128])
                wl = SY.inc(self.sW[i % 2], 16)
                bank = self.P6[i % 2]
                T.wait(self.sW[i % 2], wl)
                if i == 0:
                    T.wait(self.sVEC, xlast)
                if i >= 2:
                    T.wait(self.sACT, evms[i % 2])
                for k in range(NT):
                    T.matmul(bank[:], buf[:, k * 128:(k + 1) * 128],
                             self.g4[:, k * QC:(k + 1) * QC],
                             start=(k == 0), stop=False)
                T.matmul(bank[:], self.c_bqkv[0:1, fcol:fcol + 128],
                         self.c_o512[:], start=False, stop=True)
                pm = T.inc(self.sPE)
                wld[i % 2] = pm
                ev = self.g9[:, (i % 2) * QC:(i % 2) * QC + QC]
                A.wait(self.sPE, pm)
                if i >= 2:
                    A.wait(self.sVEC, consumed[i % 2])
                    A.wait(self.sW[2], self.sW[2].n)
                A.activation(ev, bank[:], AF.Copy)
                evms[i % 2] = A.inc(self.sACT)
                sq = (self.g5 if isk else self.g3)[:, h * QC:(h + 1) * QC]
                V.wait(self.sACT, evms[i % 2])
                V.tensor_tensor(sq, ev, ev, ALU.mult)
                consumed[i % 2] = V.inc(self.sVEC)
                SY.wait(self.sACT, evms[i % 2])
                SY.dma_start(
                    out=(d["kvK"] if isk else d["qdram"])[h * 128:(h + 1) * 128, cw],
                    in_=ev)
                SY.inc(self.sW[2], 16)
            sqv = self.sVEC.n
            T.wait(self.sVEC, sqv)
            for h2 in range(16):
                T.matmul(self.P6[4][0:16, :], self.c_c255[:, 127 - h2:143 - h2],
                         self.g5[:, h2 * QC:(h2 + 1) * QC],
                         start=(h2 == 0), stop=(h2 == 15))
            T.inc(self.sPE)
            for h2 in range(16):
                T.matmul(self.P6[5][0:16, :], self.c_c255[:, 127 - h2:143 - h2],
                         self.g3[:, h2 * QC:(h2 + 1) * QC],
                         start=(h2 == 0), stop=(h2 == 15))
            T.inc(self.sPE)
            nmm = self.sPE.n
            A.wait(self.sPE, nmm)
            A.activation(self.g11[0:16, 512:1024], self.P6[4][0:16, :], AF.Copy)
            A.activation(self.g11[0:16, 0:512], self.P6[5][0:16, :], AF.Copy)
            nev = A.inc(self.sACT)
            V.wait(self.sACT, nev)
            V.reduce_max(self.g14[0:16, 302:303], self.g11[0:16, 512:1024], axis=AX.X)
            V.reduce_max(self.g14[0:16, 303:304], self.g11[0:16, 0:512], axis=AX.X)
            if c == 0:
                V.tensor_copy(self.g14[0:16, 301:302], self.g14[0:16, 302:303])
                V.tensor_copy(self.g14[0:16, 300:301], self.g14[0:16, 303:304])
            else:
                V.tensor_tensor(self.g14[0:16, 301:302], self.g14[0:16, 301:302],
                                self.g14[0:16, 302:303], ALU.max)
                V.tensor_tensor(self.g14[0:16, 300:301], self.g14[0:16, 300:301],
                                self.g14[0:16, 303:304], ALU.max)
            V.inc(self.sVEC)
            vth = [0, 0, 0, 0]
            for nch in range(4):
                vb = 0
                for k in range(NT):
                    idx = nch * NT + k
                    if idx >= 4:
                        SY.wait(self.sPE, vth[idx % 4])
                    else:
                        SY.wait(self.sPE, wld[0])
                        SY.wait(self.sPE, wld[1])
                    SY.dma_start(out=self.g7[:, (idx % 4) * 512:(idx % 4) * 512 + 512],
                                 in_=d["Wqkvb"][k * 128:(k + 1) * 128,
                                                4096 + nch * 512:4096 + (nch + 1) * 512])
                    vl = SY.inc(self.sW[idx % 4], 16)
                    T.wait(self.sW[idx % 4], vl)
                    if idx == 0:
                        T.wait(self.sACT, evms[0])
                        T.wait(self.sACT, evms[1])
                    if k == 0 and nch > 0:
                        T.wait(self.sVEC, vpe)
                    for qs in range(4):
                        T.matmul(self.P6[qs][:],
                                 self.g4[:, k * QC + qs * 128:k * QC + qs * 128 + 128],
                                 self.g7[:, (idx % 4) * 512:(idx % 4) * 512 + 512],
                                 start=(k == 0), stop=False)
                    vth[idx % 4] = T.inc(self.sPE)
                for qs in range(4):
                    T.matmul(self.P6[qs][:], self.c_onesr[:],
                             self.c_bvb[0:1, nch * 512:(nch + 1) * 512],
                             start=False, stop=True)
                vm = T.inc(self.sPE)
                V.wait(self.sPE, vm)
                for qs in range(4):
                    V.tensor_copy(self.g3[:, qs * 2048 + nch * 512:qs * 2048 + (nch + 1) * 512],
                                  self.P6[qs][:])
                vpe = V.inc(self.sVEC)
            for qs in range(4):
                SY.wait(self.sVEC, vpe)
                SY.wait(self.sW[2], self.sW[2].n)
                SY.dma_start(out=d["kvV"][c * QC + qs * 128:c * QC + qs * 128 + 128, :],
                             in_=self.g3[:, qs * 2048:(qs + 1) * 2048])
                SY.inc(self.sW[2], 16)
            P.barrier()
            self.plans.append((f"bqkv{c}", P))

        P = Plan(self)
        T, V, A = P.eng("tensor"), P.eng("vector"), P.eng("scalar")
        SY, GP = P.eng("sync"), P.eng("gpsimd")
        SY.dma_start(out=d["mkq"][0:16, :], in_=self.g14[0:16, 301:302])
        SY.inc(self.sD, 16)
        SY.dma_start(out=d["mkq"][16:32, :], in_=self.g14[0:16, 300:301])
        mq = SY.inc(self.sD, 16)
        groups = [[2 * i, 2 * i + 1] for i in range(4)]
        GP.wait(self.sD, mq)
        GP.wait(self.sW[2], self.sW[2].n)
        GP.collective_compute("AllGather", ALU.bypass, replica_groups=groups,
                              ins=[d["mkq"].opt()], outs=[d["mkqg"].opt()])
        GP.inc(self.sCC, 1)
        GP.collective_compute("AllGather", ALU.bypass, replica_groups=groups,
                              ins=[d["kvK"].opt()], outs=[d["kvKg"].opt()])
        GP.inc(self.sCC, 1)
        GP.collective_compute("AllGather", ALU.bypass, replica_groups=groups,
                              ins=[d["kvV"].opt()], outs=[d["kvVg"].opt()])
        kvc = GP.inc(self.sCC, 1)
        SY.wait(self.sCC, kvc)
        SY.dma_start(out=self.g11[0:16, 0:1], in_=d["mkqg"][16:32, :])
        SY.inc(self.sD, 16)
        SY.dma_start(out=self.g11[0:16, 1:2], in_=d["mkqg"][48:64, :])
        mk2 = SY.inc(self.sD, 16)
        V.wait(self.sD, mk2)
        V.tensor_tensor(self.g11[0:16, 2:3], self.g11[0:16, 0:1],
                        self.g11[0:16, 1:2], ALU.max)
        V.tensor_tensor(self.g11[0:16, 2:3], self.g11[0:16, 2:3],
                        self.g14[0:16, 301:302], ALU.mult)
        bb = V.inc(self.sVEC)
        A.wait(self.sVEC, bb)
        A.activation(self.g11[0:16, 3:4], self.g11[0:16, 2:3], AF.Sqrt)
        bsq = A.inc(self.sACT)
        V.wait(self.sACT, bsq)
        V.tensor_scalar(self.g11[0:16, 4:5], self.g11[0:16, 3:4], -SCALE_B, None,
                        ALU.mult)
        bneg = V.inc(self.sVEC)
        SY.wait(self.sVEC, bneg)
        SY.dma_start(out=self.g12[0:1, 1024:1040],
                     in_=self.g11[0:16, 4:5],
                     allow_slow_non_contiguous=True)
        brow = SY.inc(self.sD, 16)
        V.wait(self.sD, brow)
        V.tensor_copy(self.g13[0:1, 0:16], self.g12[0:1, 1024:1040])
        brb = V.inc(self.sVEC)
        T.wait(self.sVEC, brb)
        T.matmul(self.P6[3][:, 0:16], self.c_onesr[:], self.g13[0:1, 0:16],
                 start=True, stop=True)
        bm = T.inc(self.sPE)
        A.wait(self.sPE, bm)
        A.activation(self.g14[:, 304:320], self.P6[3][:, 0:16], AF.Copy)
        self.bnegms = A.inc(self.sACT)
        P.barrier()
        self.plans.append(("bkv", P))

        P = Plan(self)
        T, V, A = P.eng("tensor"), P.eng("vector"), P.eng("scalar")
        SY, GP = P.eng("sync"), P.eng("gpsimd")
        scol = lambda g: (g % 2) * 2048 + (g // 2) * 128
        hfree = [0, 0]
        yst = 0
        ynm_prev = 0
        kqlh = [None, None]
        def loads(h):
            hb = h % 2
            kbuf = self.g5 if hb == 0 else self.g3
            sK = self.sW[0] if hb == 0 else self.sH[0]
            sV = self.sW[1] if hb == 0 else self.sH[1]
            sQ = self.sW[2] if hb == 0 else self.sH[2]
            SY.wait(self.sPE, hfree[hb])
            for r in range(2):
                SY.dma_start(out=kbuf[:, r * 2048:(r + 1) * 2048],
                             in_=d["kvKg"][r * 2048 + h * 128:r * 2048 + (h + 1) * 128, :])
                SY.inc(sK, 16)
            for g in range(32):
                SY.dma_start(
                    out=kbuf[:, 4096 + scol(g):4096 + scol(g) + 128],
                    in_=d["kvVg"][(g % 2) * 2048 + (g // 2) * 128:
                                  (g % 2) * 2048 + (g // 2) * 128 + 128,
                                  h * 128:(h + 1) * 128])
                SY.inc(sV, 16)
            SY.dma_start(out=self.g7[:, hb * 2048:hb * 2048 + 2048],
                         in_=d["qdram"][h * 128:(h + 1) * 128, :])
            SY.inc(sQ, 16)
            kqlh[hb] = ((sK, sK.n), (sV, sV.n), (sQ, sQ.n))
        loads(0)
        for h in range(NH):
            hb = h % 2
            kbuf = self.g5 if hb == 0 else self.g3
            qbuf = self.g7[:, hb * 2048:hb * 2048 + 2048]
            kql = kqlh[hb]
            if h + 1 < NH:
                loads(h + 1)
            bcol = self.g14[:, 304 + h:305 + h]
            for qc in range(4):
                qw = slice(qc * QC, (qc + 1) * QC)
                gmax = min(31, 8 * qc + 7)
                pexp = [0, 0]
                pvdone = [0, 0]
                mdone = [0, 0]
                for g in range(gmax + 2):
                    if g <= gmax:
                        jstart = max(0, g // 2 - 4 * qc)
                        qlo = jstart * 128
                        sb = self.P6[g % 2]
                        T.wait(*kql[0])
                        T.wait(*kql[2])
                        if h == 0 and qc == 0 and g == 0:
                            T.wait(self.sACT, self.bnegms)
                        if g >= 2:
                            T.wait(self.sACT, pexp[g % 2])
                        T.matmul(sb[:, qlo:512],
                                 kbuf[:, scol(g):scol(g) + 128],
                                 qbuf[:, qc * QC + qlo:(qc + 1) * QC],
                                 start=True, stop=True)
                        sm = T.inc(self.sPE)
                        pt = self.g9[:, (g % 2) * QC:(g % 2) * QC + QC]
                        A.wait(self.sPE, sm)
                        if g >= 2:
                            A.wait(self.sPE, pvdone[g % 2])
                        A.activation(pt[:, qlo:512], sb[:, qlo:512], AF.Exp,
                                     bias=bcol, scale=SCALE_B)
                        pexp[g % 2] = A.inc(self.sACT)
                        mdone[g % 2] = 0
                        if 0 <= g // 2 - 4 * qc < 4:
                            jloc = g // 2 - 4 * qc
                            V.wait(self.sACT, pexp[g % 2])
                            V.tensor_tensor(pt[:, jloc * 128:(jloc + 1) * 128],
                                            pt[:, jloc * 128:(jloc + 1) * 128],
                                            (self.c_m1 if g % 2 == 0 else self.c_m2)[:],
                                            ALU.mult)
                            mdone[g % 2] = V.inc(self.sVEC)
                    gp_ = g - 1
                    if 0 <= gp_ <= gmax:
                        jstart = max(0, gp_ // 2 - 4 * qc)
                        qlo = jstart * 128
                        pt = self.g9[:, (gp_ % 2) * QC:(gp_ % 2) * QC + QC]
                        T.wait(self.sACT, pexp[gp_ % 2])
                        if mdone[gp_ % 2]:
                            T.wait(self.sVEC, mdone[gp_ % 2])
                        T.wait(*kql[1])
                        if gp_ == 0 and ynm_prev:
                            T.wait(self.sVEC, ynm_prev)
                        T.matmul(self.P6[2][:, qlo:512],
                                 kbuf[:, 4096 + scol(gp_):4096 + scol(gp_) + 128],
                                 pt[:, qlo:512],
                                 start=(gp_ == 0), stop=(gp_ == gmax))
                        T.matmul(self.P7[0:1, qlo:512], self.c_ones[:],
                                 pt[:, qlo:512],
                                 start=(gp_ == 0), stop=(gp_ == gmax))
                        pvdone[gp_ % 2] = T.inc(self.sPE)
                pvm = pvdone[gmax % 2]
                A.wait(self.sPE, pvm)
                A.activation(self.g11[0:1, 512:1024], self.P7[0:1, :], AF.Copy)
                csv = A.inc(self.sACT)
                V.wait(self.sACT, csv)
                V.reciprocal(self.g11[0:1, 512:1024], self.g11[0:1, 512:1024])
                V.tensor_copy(self.g13[0:1, 16:528], self.g11[0:1, 512:1024])
                rcb = V.inc(self.sVEC)
                T.wait(self.sVEC, rcb)
                T.matmul(self.P6[3][:], self.c_onesr[:], self.g13[0:1, 16:528],
                         start=True, stop=True)
                bcm = T.inc(self.sPE)
                A.wait(self.sPE, bcm)
                A.activation(self.g11[:, 0:512], self.P6[3][:], AF.Copy)
                bce = A.inc(self.sACT)
                V.wait(self.sACT, bce)
                if yst:
                    V.wait(self.sW[3], yst)
                V.tensor_tensor(self.g15[:, 0:512], self.P6[2][:],
                                self.g11[:, 0:512], ALU.mult)
                ynm = V.inc(self.sVEC)
                ynm_prev = ynm
                SY.wait(self.sVEC, ynm)
                SY.dma_start(out=d["ydram"][h * 128:(h + 1) * 128, qw],
                             in_=self.g15[:, 0:512])
                yst = SY.inc(self.sW[3], 16)
            hfree[hb] = self.sPE.n
        P.barrier()
        self.plans.append(("batt", P))

        for c in range(NQC):
            P = Plan(self)
            T, V, A = P.eng("tensor"), P.eng("vector"), P.eng("scalar")
            SY, GP = P.eng("sync"), P.eng("gpsimd")
            cw = slice(c * QC, (c + 1) * QC)
            SY.dma_start(out=self.g4.ap().rearrange("p (k q) -> p k q", k=NT),
                         in_=d["ydram"].rearrange("(k p) q -> p k q", p=128)[:, :, cw])
            yl = SY.inc(self.sD, 16)
            wpm = [0, 0]
            aev = [0, 0]
            for ft in range(NH):
                if ft >= 2:
                    SY.wait(self.sPE, wpm[ft % 2])
                buf = self.g7[:, (ft % 2) * 2048:(ft % 2) * 2048 + 2048]
                SY.dma_start(
                    out=buf.rearrange("p (k f) -> p k f", k=NT),
                    in_=d["Woutb"].rearrange("(k p) f -> p k f", p=128)[:, :, ft * 128:(ft + 1) * 128])
                wl = SY.inc(self.sW[ft % 2], 16)
                bank = self.P6[ft % 2]
                T.wait(self.sW[ft % 2], wl)
                if ft < 2:
                    T.wait(self.sD, yl)
                if ft >= 2:
                    T.wait(self.sACT, aev[ft % 2])
                for k in range(NT):
                    T.matmul(bank[:], buf[:, k * 128:(k + 1) * 128],
                             self.g4[:, k * QC:(k + 1) * QC],
                             start=(k == 0), stop=False)
                T.matmul(bank[:], self.c_bout[0:1, ft * 128:(ft + 1) * 128],
                         self.c_o512[:], start=False, stop=True)
                wpm[ft % 2] = T.inc(self.sPE)
                A.wait(self.sPE, wpm[ft % 2])
                if ft == 0:
                    A.wait(self.sDG, self.sDG.n)
                A.activation(self.g2[:, ft * QC:(ft + 1) * QC], bank[:], AF.Copy)
                aev[ft % 2] = A.inc(self.sACT)
            GP.wait(self.sACT, aev[(NH - 1) % 2])
            GP.wait(self.sACT, aev[(NH - 2) % 2])
            GP.dma_start(out=self.out_r[:, :, cw],
                         in_=self.g2[:, 0:8192].rearrange("p (k q) -> p k q", k=NT),
                         accum_op=ALU.add)
            GP.inc(self.sDG, 16)
            P.barrier()
            self.plans.append((f"bwout{c}", P))

    def finish(self):
        P = Plan(self)
        P.barrier()
        for e in ENGINES:
            P.eng(e).wait(self.sDG, self.sDG.n)
            P.eng(e).wait(self.sDS, self.sDS.n)
            P.eng(e).wait(self.sCAST, self.sCAST.n)
            P.eng(e).wait(self.sCAST2, self.sCAST2.n)
        self.plans.append(("fin", P))


def kernel(**inputs):
    maps = _prepare_inputs(inputs)
    k = K()
    nc = k.build()
    res = run_bass_kernel_spmd(nc, maps, core_ids=list(range(NCORES)))
    return _assemble(res.results)


# version 07 (final)
# speedup vs baseline: 1.0350x; 1.0350x over previous
import math
import numpy as np
import ml_dtypes
from contextlib import ExitStack

import concourse.bass as bass
import concourse.mybir as mybir
from concourse.bass import IndirectOffsetOnAxis
from concourse.bass_utils import run_bass_kernel_spmd

F32 = mybir.dt.float32
BF16 = mybir.dt.bfloat16
I32 = mybir.dt.int32
AF = mybir.ActivationFunctionType
ALU = mybir.AluOpType
AX = mybir.AxisListType

NCORES = 8
B = 4
S = 4096
DM = 2048
TLOC = 2048
NT = 16
NQC = 4
QC = 512
NH = 16
NKV = 8
DH = 128
FE = 64
E = 8
CAP = 2560
CAPP = 2688
NGT = 21
AG2R = 2816
ZROW = 2688
MAGIC = 1.5 * 2.0 ** 23
INV2PI = 1.0 / (2.0 * math.pi)
TWOPI = 2.0 * math.pi
LN_EPS = 1e-5
SCALE_B = 1.0 / math.sqrt(DH)
ENGINES = ("sync", "tensor", "scalar", "vector", "gpsimd")
UPTO = 99
SHARED_AG = True


class Sem:
    def __init__(self, nc, es, name):
        self.h = es.enter_context(nc.semaphore(name))
        self.n = 0


class _Rec:

    def __init__(self, plan, eng, selfsem=None):
        self._p = plan
        self._e = eng
        self._ss = selfsem

    def wait(self, sem, val):
        if val > 0:
            self._p.ops[self._e].append(("wait", sem, val))

    def inc(self, sem, k=1):
        if sem is self._ss:
            return sem.n
        self._p.ops[self._e].append(("inc", sem, k))
        sem.n += k
        return sem.n

    DMA_OPS = {"dma_start", "indirect_dma_start", "collective_compute",
               "dma_start_transpose"}

    def op(self, name, *args, **kw):
        if self._ss is not None and name not in self.DMA_OPS:
            if self._ss.n > 0:
                self._p.ops[self._e].append(("wait", self._ss, self._ss.n))
            self._p.ops[self._e].append(("op", name, args, kw))
            self._p.ops[self._e].append(("inc", self._ss, 1))
            self._ss.n += 1
        else:
            self._p.ops[self._e].append(("op", name, args, kw))

    def __getattr__(self, name):
        def f(*args, **kw):
            self.op(name, *args, **kw)
        return f


class Plan:
    def __init__(self, k):
        self.k = k
        self.ops = {e: [] for e in ENGINES}
        ss = {"vector": k.sVEC, "scalar": k.sACT, "gpsimd": k.sGP}
        self.r = {e: _Rec(self, e, ss.get(e)) for e in ENGINES}

    def eng(self, name):
        return self.r[name]

    def barrier(self):
        k = self.k
        sems = [s for s in k.all_sems if s.n > 0]
        for e in ENGINES:
            for s in sems:
                self.r[e].wait(s, s.n)

    def run(self, nc, name):
        with nc.Block(name) as blk:
            for eng in ENGINES:
                ops = self.ops[eng]
                if not ops:
                    continue

                def body(e, ops=ops):
                    last = None
                    for item in ops:
                        if item[0] == "wait":
                            e.wait_ge(item[1].h, item[2])
                        elif item[0] == "inc":
                            assert last is not None
                            last.then_inc(item[1].h, item[2])
                        else:
                            _, nm, args, kw = item
                            last = getattr(e, nm)(*args, **kw)

                getattr(blk, eng)(body)


def _bf(x):
    return np.asarray(x, dtype=ml_dtypes.bfloat16)


def _core_tables(core):
    par = core % 2
    b = core // 2
    loc = np.arange(TLOC)
    j = loc // 128
    pos = (2 * j + par) * 128 + (loc % 128)
    inv_freq = 1.0 / (10000.0 ** (np.arange(0, DH, 2, dtype=np.float64) / DH))
    ang = pos[None, :].astype(np.float64) * inv_freq[:, None]
    cs = np.zeros((128, 2 * TLOC), dtype=np.float32)
    cs[0:64, 0:TLOC] = np.cos(ang); cs[64:128, 0:TLOC] = np.cos(ang)
    cs[0:64, TLOC:] = np.sin(ang); cs[64:128, TLOC:] = np.sin(ang)

    i = np.arange(128)
    tri = (i[:, None] <= i[None, :]).astype(np.float32)
    if par == 0:
        m1, m2 = tri, np.zeros((128, 128), np.float32)
    else:
        m1, m2 = np.ones((128, 128), np.float32), tri

    gidx = np.zeros((128, NT), dtype=np.float32)
    for jj in range(NT):
        l = jj * 128 + np.arange(128)
        gidx[:, jj] = (l // 512) * (NCORES * 512) + core * 512 + (l % 512)

    trimy = np.zeros((128, NT), dtype=np.float32)
    for jj in range(NT):
        gmine = b * 32 + 2 * jj + par
        for r in range(128):
            coreR, jR = r // 16, r % 16
            gR = (coreR // 2) * 32 + 2 * jR + (coreR % 2)
            if gR < gmine:
                trimy[r, jj] = 1.0

    idxscr = np.zeros((128, NT), dtype=np.int32)
    for jj in range(NT):
        for ii in range(128):
            gg = 16 * par + ii // 8
            r = gg % 2
            idxscr[ii, jj] = (r * 16 + jj) * 128 + (gg // 2) * 8 + (ii % 8)
    trimyr = np.repeat(trimy.reshape(128, NT, 1), 128, axis=2).reshape(128, NT * 128)
    return dict(cos_sin=cs, m1=_bf(m1), m2=_bf(m2), gidx=gidx, trimy=_bf(trimy),
                idxscr=idxscr, trimyr=_bf(trimyr))


def _const_tables():
    i = np.arange(128)
    ident = _bf(np.eye(128, dtype=np.float32))
    tri_incl = _bf((i[:, None] <= i[None, :]).astype(np.float32))
    c255 = np.zeros((128, 255), dtype=np.float32)
    c255[:, 127] = 1.0
    iota_e = np.tile(np.arange(E, dtype=np.float32)[None, :], (128, 1))
    constp = np.zeros((128, 8), dtype=np.float32)
    constp[:, 0] = LN_EPS
    constp[:, 1] = 1e-6
    constp[:, 2] = -1e9
    constp[:, 3] = 12544.0
    prot = np.zeros((128, 128), dtype=np.float32)
    for m in range(64):
        prot[m + 64, m] = -1.0
        prot[m, m + 64] = 1.0
    return dict(ident=ident, tri_incl=tri_incl, c255=_bf(c255), iota_e=iota_e,
                constp=constp, prot=_bf(prot),
                ones_col=_bf(np.ones((128, 1), np.float32)),
                ones_row=_bf(np.ones((1, 128), np.float32)),
                ones512=_bf(np.ones((1, 512), np.float32)),
                neg8=_bf(-np.ones((1, 8), np.float32)))


def _prepare_inputs(inp):
    x = np.asarray(inp["L0_attn_a_hidden_states"], dtype=np.float32)
    co = _const_tables()
    f32c = lambda a: np.ascontiguousarray(np.asarray(a), dtype=np.float32)
    maps = []
    for core in range(NCORES):
        par, b = core % 2, core // 2
        tiles = x[b].reshape(32, 128, DM)[par::2]
        xT = np.ascontiguousarray(tiles.reshape(TLOC, DM).T)
        t = _core_tables(core)
        m = dict(
            xT=xT,
            WqT=f32c(np.asarray(inp["L0_attn_a_Wq"]).T),
            WkT=f32c(np.asarray(inp["L0_attn_a_Wk"]).T),
            WvT=f32c(np.asarray(inp["L0_attn_a_Wv"]).T),
            WoT=f32c(np.asarray(inp["L0_attn_a_Wo"]).T),
            bq=f32c(np.asarray(inp["L0_attn_a_bq"]).reshape(1, 2048)),
            bk=f32c(np.asarray(inp["L0_attn_a_bk"]).reshape(1, 1024)),
            bv=f32c(np.asarray(inp["L0_attn_a_bv"]).reshape(1, NKV * 128)),
            bo=f32c(np.asarray(inp["L0_attn_a_bo"]).reshape(1, 2048)),
            RT=f32c(np.asarray(inp["L0_attn_a_rand_matrix"]).T),
            Wqkv=f32c(inp["L0_attn_b_w_qkv"]),
            bqkv=f32c(np.asarray(inp["L0_attn_b_b_qkv"])[:4096].reshape(1, 4096)),
            bvb=f32c(np.asarray(inp["L0_attn_b_b_qkv"])[4096:].reshape(1, 2048)),
            Wout=f32c(inp["L0_attn_b_w_out"]),
            bout=f32c(np.asarray(inp["L0_attn_b_b_out"]).reshape(1, 2048)),
            gT=f32c(np.asarray(inp["L0_ffn_gate_weight"]).T),
            W1=f32c(np.asarray(inp["L0_ffn_w1"])[core]),
            W2=f32c(np.asarray(inp["L0_ffn_w2"])[core]),
            cos_sin=t["cos_sin"], m1=t["m1"], m2=t["m2"], gidx=t["gidx"],
            idxscr=t["idxscr"], trimyr=t["trimyr"],
            trimy=t["trimy"], ident=co["ident"], tri_incl=co["tri_incl"],
            c255=co["c255"], iota_e=co["iota_e"], ones_col=co["ones_col"],
            ones_row=co["ones_row"], ones512=co["ones512"], neg8=co["neg8"],
            constp=co["constp"], prot=co["prot"],
        )
        maps.append(m)
    return maps


def _assemble(results):
    out = np.empty((B, S, DM), dtype=np.float32)
    for core in range(NCORES):
        par, b = core % 2, core // 2
        xT = np.asarray(results[core]["out"])
        out[b].reshape(32, 128, DM)[par::2] = xT.T.reshape(NT, 128, DM)
    return out


class K:
    def __init__(self, upto=UPTO):
        self.upto = upto
        nc = bass.Bass("TRN2", target_bir_lowering=False,
                       dynamic_dma_scratch_size=16384)
        self.nc = nc
        self.es = ExitStack()
        d = self.d = {}

        def din(name, shape, dtype=F32):
            d[name] = nc.dram_tensor(name, shape, dtype, kind="ExternalInput").ap()

        din("xT", [DM, TLOC])
        din("WqT", [DM, 2048]); din("WkT", [DM, 1024]); din("WvT", [DM, 1024])
        din("WoT", [DM, 2048])
        din("bq", [1, 2048]); din("bk", [1, 1024]); din("bv", [1, NKV * 128])
        din("bo", [1, 2048]); din("RT", [128, FE])
        din("Wqkv", [DM, 3 * DM]); din("bqkv", [1, 4096]); din("bvb", [1, 2048])
        din("Wout", [DM, DM]); din("bout", [1, 2048])
        din("gT", [DM, E])
        din("W1", [DM, DM]); din("W2", [DM, DM])
        din("cos_sin", [128, 2 * TLOC])
        din("m1", [128, 128], BF16); din("m2", [128, 128], BF16)
        din("gidx", [128, NT]); din("trimy", [128, NT], BF16)
        din("idxscr", [128, NT], I32)
        din("trimyr", [128, NT * 128], BF16)
        din("ident", [128, 128], BF16); din("tri_incl", [128, 128], BF16)
        din("c255", [128, 255], BF16); din("iota_e", [128, E])
        din("ones_col", [128, 1], BF16); din("ones_row", [1, 128], BF16)
        din("constp", [128, 8]); din("prot", [128, 128], BF16)
        din("ones512", [1, 512], BF16); din("neg8", [1, 8], BF16)

        self.out = nc.dram_tensor("out", [DM, TLOC], F32, kind="ExternalOutput").ap()
        self.out_r = self.out.rearrange("(k p) q -> p k q", p=128)

        def dint(name, shape, dtype, shared=False):
            kw = dict(addr_space="Shared") if shared else {}
            d[name] = nc.dram_tensor(name, shape, dtype, **kw).ap()

        dint("WqTb", [DM, 2048], BF16); dint("WkTb", [DM, 1024], BF16)
        dint("WvTb", [DM, 1024], BF16); dint("WoTb", [DM, 2048], BF16)
        dint("Wqkvb", [DM, 3 * DM], BF16); dint("Woutb", [DM, DM], BF16)
        dint("W1b", [DM, DM], BF16); dint("W2b", [DM, DM], BF16)
        dint("ag1in", [TLOC, DM], BF16)
        dint("ag1out", [NCORES * TLOC, DM], BF16, shared=SHARED_AG)
        dint("ag2in", [AG2R, DM], BF16)
        dint("ag2out", [NCORES * AG2R, DM], BF16, shared=SHARED_AG)
        dint("fragi", [E * CAPP, 1], F32); dint("fragw", [E * CAPP, 1], F32)
        dint("fragio", [CAPP, 1], F32); dint("fragwo", [CAPP, 1], F32)
        dint("cntin", [NT, E], F32); dint("cntout", [NCORES * NT, E], F32)
        dint("kvK", [DM, TLOC], BF16)
        dint("kvV", [TLOC, DM], BF16)
        dint("kvKg", [2 * DM, TLOC], BF16)
        dint("kvVg", [2 * TLOC, DM], BF16)
        dint("mkq", [32, 1], F32); dint("mkqg", [64, 1], F32)
        dint("qdram", [DM, TLOC], BF16)
        dint("ydram", [DM, TLOC], BF16)
        dint("mixh", [NH * TLOC, 128], BF16)
        dint("mixhg", [2 * NH * TLOC, 128], BF16)

        es = self.es
        self.sD = Sem(nc, es, "sD")
        self.sDS = Sem(nc, es, "sDS")
        self.sDG = Sem(nc, es, "sDG")
        self.sPE = Sem(nc, es, "sPE")
        self.sACT = Sem(nc, es, "sACT")
        self.sVEC = Sem(nc, es, "sVEC")
        self.sCC = Sem(nc, es, "sCC")
        self.sGP = Sem(nc, es, "sGP")
        self.sW = [Sem(nc, es, f"sW{i}") for i in range(4)]
        self.sG = [Sem(nc, es, f"sG{i}") for i in range(4)]
        self.sH = [Sem(nc, es, f"sH{i}") for i in range(3)]
        self.sCAST = Sem(nc, es, "sCAST")
        self.sCAST2 = Sem(nc, es, "sCAST2")
        self.sSC = Sem(nc, es, "sSC")
        self.all_sems = [self.sD, self.sPE, self.sACT,
                         self.sVEC, self.sGP, self.sSC] + self.sW + self.sG + self.sH

        sb = lambda name, shape, dt: es.enter_context(nc.sbuf_tensor(name, shape, dt))
        self.c_ident = sb("c_ident", [128, 128], BF16)
        self.c_tri = sb("c_tri", [128, 128], BF16)
        self.c_c255 = sb("c_c255", [128, 255], BF16)
        self.c_trimy = sb("c_trimy", [128, NT], BF16)
        self.c_m1 = sb("c_m1", [128, 128], BF16)
        self.c_m2 = sb("c_m2", [128, 128], BF16)
        self.c_iota = sb("c_iota", [128, E], F32)
        self.c_gidx = sb("c_gidx", [128, NT], F32)
        self.c_idxscr = sb("c_idxscr", [128, NT], I32)
        self.c_trimyr = sb("c_trimyr", [128, NT * 128], BF16)
        self.c_cri = sb("c_cri", [128, NT], I32)
        self.c_sci = sb("c_sci", [128, 2 * NGT + 2], I32)
        self.c_ones = sb("c_ones", [128, 1], BF16)
        self.c_onesr = sb("c_onesr", [1, 128], BF16)
        self.c_constp = sb("c_constp", [128, 8], F32)
        self.c_prot = sb("c_prot", [128, 128], BF16)
        self.c_csb = sb("c_csb", [128, 2 * TLOC], BF16)
        self.c_rt = sb("c_rt", [128, FE], BF16)
        self.c_gT = sb("c_gT", [128, NT * E], BF16)
        self.c_bq = sb("c_bq", [1, 2048], BF16)
        self.c_bk = sb("c_bk", [1, 1024], BF16)
        self.c_bo = sb("c_bo", [1, 2048], BF16)
        self.c_bv = sb("c_bv", [1, NKV * 128], BF16)
        self.c_bqkv = sb("c_bqkv", [1, 4096], BF16)
        self.c_bvb = sb("c_bvb", [1, 2048], BF16)
        self.c_bout = sb("c_bout", [1, 2048], BF16)
        self.c_o512 = sb("c_o512", [1, 512], BF16)
        self.c_nego = sb("c_nego", [1, 8], BF16)

        self.g1 = sb("g1", [128, 8192], F32)
        self.g2 = sb("g2", [128, 8192], F32)
        self.g3 = sb("g3", [128, 8192], BF16)
        self.g4 = sb("g4", [128, 8192], BF16)
        self.g5 = sb("g5", [128, 8192], BF16)
        self.g6 = sb("g6", [128, 4096], BF16)
        self.g7 = sb("g7", [128, 4096], BF16)
        self.g8 = sb("g8", [128, 3072], BF16)
        self.g9 = sb("g9", [128, 2048], BF16)
        self.g10 = sb("g10", [128, 1024], F32)
        self.g11 = sb("g11", [128, 1024], F32)
        self.g12 = sb("g12", [128, 1536], F32)
        self.g13 = sb("g13", [128, 1536], BF16)
        self.g14 = sb("g14", [128, 640], F32)
        self.g15 = sb("g15", [128, 2048], BF16)

        ps = lambda name, shape, dt: es.enter_context(nc.psum_tensor(name, shape, dt))
        self.P6 = [ps(f"P{i}", [128, 512], F32) for i in range(6)]
        self.PT = ps("PT", [128, 1024], BF16)
        self.P7 = ps("P7", [128, 512], F32)

        self.plans = []

    def build(self):
        with self.es:
            self.phase0()
            if self.upto >= 1:
                self.attn_a()
                self.attna_fin()
            if self.upto >= 2:
                self.ffn(0)
            if self.upto >= 3:
                self.attn_b()
            if self.upto >= 4:
                self.ffn(1)
            self.finish()
            import os
            nplans = int(os.environ.get("KNPLANS", "9999"))
            if os.environ.get("KSPLIT"):
                for name, P in self.plans[:nplans]:
                    P.run(self.nc, name)
            else:
                merged = Plan(self)
                for name, P in self.plans[:nplans]:
                    for e in ENGINES:
                        merged.ops[e].extend(P.ops[e])
                merged.run(self.nc, "mega")
        return self.nc

    def phase0(self):
        P = Plan(self)
        d = self.d
        SY, GP, V = P.eng("sync"), P.eng("gpsimd"), P.eng("vector")

        for tile, name in [
            (self.c_ident, "ident"), (self.c_tri, "tri_incl"),
            (self.c_c255, "c255"), (self.c_trimy, "trimy"), (self.c_m1, "m1"),
            (self.c_m2, "m2"), (self.c_iota, "iota_e"), (self.c_gidx, "gidx"),
            (self.c_ones, "ones_col"), (self.c_onesr, "ones_row"),
            (self.c_idxscr, "idxscr"), (self.c_trimyr, "trimyr"),
            (self.c_constp, "constp"), (self.c_prot, "prot"),
            (self.c_o512, "ones512"), (self.c_nego, "neg8"),
        ]:
            SY.dma_start(out=tile[:], in_=d[name]); SY.inc(self.sD, 16)

        for tile, name in [(self.c_csb, "cos_sin"), (self.c_rt, "RT"),
                           (self.c_bv, "bv"), (self.c_bvb, "bvb"),
                           (self.c_bq, "bq"), (self.c_bk, "bk"),
                           (self.c_bo, "bo"), (self.c_bqkv, "bqkv"),
                           (self.c_bout, "bout")]:
            GP.dma_start(out=tile[:], in_=d[name]); GP.inc(self.sDG, 16)
        GP.dma_start(out=self.c_gT.ap().rearrange("p (k e) -> p k e", e=E),
                     in_=d["gT"].rearrange("(k p) e -> p k e", p=128))
        GP.inc(self.sDG, 16)
        for s_, dst in [("WqT", "WqTb"), ("WkT", "WkTb"), ("WvT", "WvTb")]:
            GP.dma_start(out=d[dst], in_=d[s_]); GP.inc(self.sCAST, 16)
        early = self.sCAST.n
        for s_, dst in [("WoT", "WoTb"), ("Wqkv", "Wqkvb"), ("Wout", "Woutb"),
                        ("W1", "W1b"), ("W2", "W2b")]:
            GP.dma_start(out=d[dst], in_=d[s_]); GP.inc(self.sCAST2, 16)
        self.castdone = self.sCAST2.n
        GP.dma_start(out=self.out, in_=d["xT"]); GP.inc(self.sDS, 16)

        V.memset(self.g9[:, 0:2048], 0.0); zv = V.inc(self.sVEC)
        SY.wait(self.sVEC, zv)
        SY.dma_start(out=d["ag2in"][ZROW:AG2R, :], in_=self.g9[:, 0:2048])
        SY.inc(self.sD, 16)

        self.early = early
        for e in ENGINES:
            P.eng(e).wait(self.sD, self.sD.n)
            P.eng(e).wait(self.sDG, self.sDG.n)
            P.eng(e).wait(self.sVEC, self.sVEC.n)
        self.plans.append(("p0", P))

    def ln_chunk(self, P, c, src=None):
        T, V, A, SY = P.eng("tensor"), P.eng("vector"), P.eng("scalar"), P.eng("sync")
        cw = slice(c * QC, (c + 1) * QC)
        r_m = self.g12[0:1, 0:512]
        r_v = self.g12[0:1, 512:1024]
        r_t = self.g12[0:1, 1024:1536]
        rb_m = self.g13[0:1, 0:512]
        rb_h = self.g13[0:1, 512:1024]
        rb_l = self.g13[0:1, 1024:1536]

        SY.wait(self.sDG, self.sDG.n)
        if src is None:
            src = self.out_r
            SY.wait(self.sDS, 16)
        SY.dma_start(out=self.g1.ap().rearrange("p (k q) -> p k q", k=NT),
                     in_=src[:, :, cw])
        ld = SY.inc(self.sD, 16)

        V.wait(self.sD, ld)
        sq_ms = []
        tq = [0] * NT
        for k in range(NT):
            V.tensor_copy(self.g3[:, k * QC:(k + 1) * QC],
                          self.g1[:, k * QC:(k + 1) * QC])
            V.inc(self.sVEC)
            if k >= 2:
                V.wait(self.sPE, tq[k - 2])
            V.tensor_tensor(self.g9[:, (k % 2) * QC:(k % 2) * QC + QC],
                            self.g3[:, k * QC:(k + 1) * QC],
                            self.g3[:, k * QC:(k + 1) * QC], ALU.mult)
            sq_ms.append(V.inc(self.sVEC))
            T.wait(self.sVEC, sq_ms[k])
            T.matmul(self.P7[0:1, :], self.c_ones[:],
                     self.g9[:, (k % 2) * QC:(k % 2) * QC + QC],
                     start=(k == 0), stop=(k == NT - 1))
            tq[k] = T.inc(self.sPE)
        mq = tq[NT - 1]
        A.wait(self.sPE, mq)
        A.activation(r_t, self.P7[0:1, :], AF.Copy, scale=1.0 / DM)
        aq = A.inc(self.sACT)
        for k in range(NT):
            if k == 0:
                T.wait(self.sACT, aq)
            T.matmul(self.P7[0:1, :], self.c_ones[:], self.g3[:, k * QC:(k + 1) * QC],
                     start=(k == 0), stop=(k == NT - 1))
        mx = T.inc(self.sPE)
        A.wait(self.sPE, mx)
        A.activation(r_m, self.P7[0:1, :], AF.Copy, scale=1.0 / DM)
        am = A.inc(self.sACT)

        V.wait(self.sACT, am)
        V.tensor_tensor(r_v, r_m, r_m, ALU.mult); V.inc(self.sVEC)
        V.tensor_tensor(r_v, r_t, r_v, ALU.subtract); vv = V.inc(self.sVEC)
        A.wait(self.sVEC, vv)
        A.activation(r_t, r_v, AF.Ln, bias=self.c_constp[0:1, 0:1])
        A.inc(self.sACT)
        A.activation(r_v, r_t, AF.Exp, scale=-0.5)
        ar = A.inc(self.sACT)
        V.wait(self.sACT, ar)
        V.tensor_copy(rb_h, r_v); V.inc(self.sVEC)
        V.tensor_tensor(r_t, r_v, rb_h, ALU.subtract); V.inc(self.sVEC)
        V.tensor_copy(rb_l, r_t); V.inc(self.sVEC)
        V.tensor_copy(rb_m, r_m); vr = V.inc(self.sVEC)

        T.wait(self.sVEC, vr)
        T.matmul(self.P6[0][:], self.c_onesr[:], rb_h, start=True, stop=False)
        T.matmul(self.P6[0][:], self.c_onesr[:], rb_l, start=False, stop=True)
        T.matmul(self.P6[1][:], self.c_onesr[:], rb_m, start=True, stop=True)
        tb = T.inc(self.sPE)
        V.wait(self.sPE, tb)
        V.tensor_copy(self.g11[:, 0:512], self.P6[0][:]); V.inc(self.sVEC)
        V.tensor_copy(self.g11[:, 512:1024], self.P6[1][:]); V.inc(self.sVEC)
        xl = []
        for k in range(NT):
            V.tensor_tensor(self.g9[:, (k % 2) * QC:(k % 2) * QC + QC],
                            self.g3[:, k * QC:(k + 1) * QC],
                            self.g11[:, 512:1024], ALU.subtract)
            V.inc(self.sVEC)
            V.tensor_tensor(self.g4[:, k * QC:(k + 1) * QC],
                            self.g9[:, (k % 2) * QC:(k % 2) * QC + QC],
                            self.g11[:, 0:512], ALU.mult)
            xl.append(V.inc(self.sVEC))
        return xl

    def attn_a(self):
        d = self.d
        for c in range(NQC):
            P = Plan(self)
            T, V, A = P.eng("tensor"), P.eng("vector"), P.eng("scalar")
            SY, GP = P.eng("sync"), P.eng("gpsimd")
            cw = slice(c * QC, (c + 1) * QC)
            csin = slice(TLOC + c * QC, TLOC + (c + 1) * QC)

            xl = self.ln_chunk(
                P, c, src=self.d["xT"].rearrange("(k p) q -> p k q", p=128))
            xlast = xl[NT - 1]
            if c == 0:
                SY.wait(self.sCAST, self.early)

            wsrc = [("WqTb", h, self.c_bq, h) for h in range(NH)] + \
                   [("WkTb", h, self.c_bk, h) for h in range(NKV)]
            pm_l = [0] * 24
            ev_l = [0] * 24
            rope_l = [0] * 24
            sq_l = [0] * 24
            pmm_l = [0] * 24
            pev_l = [0] * 24
            rm_l = [0] * 24
            wld = [0] * 24

            def s1(i):
                wn, h, btile, bcol = wsrc[i]
                if i >= 2:
                    SY.wait(self.sPE, pm_l[i - 2])
                buf = self.g7[:, (i % 2) * 2048:(i % 2) * 2048 + 2048]
                SY.dma_start(
                    out=buf.rearrange("p (k f) -> p k f", k=NT),
                    in_=d[wn].rearrange("(k p) f -> p k f", p=128)[:, :, h * 128:(h + 1) * 128])
                wld[i] = (self.sW[i % 2], SY.inc(self.sW[i % 2], 16))
                bank = self.P6[i % 2]
                T.wait(*wld[i])
                if i == 0:
                    T.wait(self.sVEC, xlast)
                if i >= 2:
                    T.wait(self.sACT, ev_l[i - 2])
                for k in range(NT):
                    T.matmul(bank[:], buf[:, k * 128:(k + 1) * 128],
                             self.g4[:, k * QC:(k + 1) * QC],
                             start=(k == 0), stop=False)
                T.matmul(bank[:], btile[0:1, bcol * 128:(bcol + 1) * 128],
                         self.c_o512[:], start=False, stop=True)
                pm_l[i] = T.inc(self.sPE)
                qb = self.g9[:, (i % 2) * QC:(i % 2) * QC + QC]
                A.wait(self.sPE, pm_l[i])
                if i >= 2:
                    A.wait(self.sVEC, rope_l[i - 2])
                A.activation(qb, bank[:], AF.Copy)
                ev_l[i] = A.inc(self.sACT)

            def s2(i):
                rbank = self.P6[2 + i % 2]
                qb = self.g9[:, (i % 2) * QC:(i % 2) * QC + QC]
                T.wait(self.sACT, ev_l[i])
                if i >= 2:
                    T.wait(self.sVEC, rope_l[i - 2])
                T.matmul(rbank[:], self.c_prot[:], qb, start=True, stop=True)
                rm_l[i] = T.inc(self.sPE)
                ro = self.g5[:, (i % 2) * QC:(i % 2) * QC + QC]
                t1 = self.g9[:, 1024 + (i % 2) * QC:1024 + (i % 2) * QC + QC]
                V.wait(self.sPE, rm_l[i])
                if i >= 2:
                    V.wait(self.sPE, pmm_l[i - 2])
                V.tensor_tensor(t1, qb, self.c_csb[:, cw], ALU.mult)
                V.inc(self.sVEC)
                V.tensor_tensor(ro, rbank[:], self.c_csb[:, csin], ALU.mult)
                V.inc(self.sVEC)
                V.tensor_tensor(ro, ro, t1, ALU.add)
                rope_l[i] = V.inc(self.sVEC)
                sqb = self.g15[:, (i % 2) * QC:(i % 2) * QC + QC]
                V.tensor_tensor(sqb, ro, ro, ALU.mult)
                sq_l[i] = V.inc(self.sVEC)

            def s3(i):
                ro = self.g5[:, (i % 2) * QC:(i % 2) * QC + QC]
                sqb = self.g15[:, (i % 2) * QC:(i % 2) * QC + QC]
                T.wait(self.sVEC, sq_l[i])
                if i >= 1:
                    T.wait(self.sACT, pev_l[i - 1])
                for qs in range(4):
                    T.matmul(self.P6[4][:, qs * 64:(qs + 1) * 64],
                             ro[:, qs * 128:(qs + 1) * 128], self.c_rt[:],
                             start=True, stop=True)
                for qs in range(4):
                    T.matmul(self.P6[5][:, (qs * 24 + i):(qs * 24 + i) + 1],
                             sqb[:, qs * 128:(qs + 1) * 128], self.c_ones[:],
                             start=True, stop=True)
                pmm_l[i] = T.inc(self.sPE)
                A.wait(self.sPE, pmm_l[i])
                for qs in range(4):
                    A.activation(self.g1[:, qs * 1536 + i * 64:qs * 1536 + i * 64 + 64],
                                 self.P6[4][:, qs * 64:(qs + 1) * 64], AF.Copy)
                pev_l[i] = A.inc(self.sACT)

            for step in range(26):
                if step < 24:
                    s1(step)
                if 0 <= step - 1 < 24:
                    s2(step - 1)
                if 0 <= step - 2 < 24:
                    s3(step - 2)
            ptm_last = pev_l[23]
            rope_done = [rope_l[22], rope_l[23]]
            pmm_done = [pmm_l[22], pmm_l[23]]
            vth = [0] * (2 * NT)
            vi = 0
            vp_ev = 0
            for nch in range(2):
                for k in range(NT):
                    idx = nch * NT + k
                    if idx >= 4:
                        SY.wait(self.sPE, vth[idx - 4])
                    else:
                        SY.wait(self.sPE, pmm_done[0])
                        SY.wait(self.sPE, pmm_done[1])
                    SY.dma_start(out=self.g7[:, (idx % 4) * 512:(idx % 4) * 512 + 512],
                                 in_=d["WvTb"][k * 128:(k + 1) * 128,
                                               nch * 512:(nch + 1) * 512])
                    vld = SY.inc(self.sW[idx % 4], 16)
                    T.wait(self.sW[idx % 4], vld)
                    if idx == 0:
                        T.wait(self.sACT, ev_l[23])
                        T.wait(self.sVEC, rope_done[0])
                        T.wait(self.sVEC, rope_done[1])
                    if k == 0 and nch == 1:
                        T.wait(self.sVEC, vp_ev)
                    for qs in range(4):
                        T.matmul(self.P6[qs][:],
                                 self.g4[:, k * QC + qs * 128:k * QC + qs * 128 + 128],
                                 self.g7[:, (idx % 4) * 512:(idx % 4) * 512 + 512],
                                 start=(k == 0), stop=False)
                    vth[idx] = T.inc(self.sPE)
                for qs in range(4):
                    T.matmul(self.P6[qs][:], self.c_onesr[:],
                             self.c_bv[0:1, nch * 512:(nch + 1) * 512],
                             start=False, stop=True)
                vb = T.inc(self.sPE)
                V.wait(self.sPE, vb)
                for qs in range(4):
                    V.tensor_copy(self.g6[:, qs * 1024 + nch * 512:qs * 1024 + nch * 512 + 512],
                                  self.P6[qs][:])
                vp_ev = V.inc(self.sVEC)

            vmm_last = self.sPE.n
            rnt = self.g14[:, 0:96]
            V.wait(self.sPE, pmm_done[0])
            V.wait(self.sPE, pmm_done[1])
            V.tensor_copy(rnt, self.P6[5][:, 0:96]); rc = V.inc(self.sVEC)
            A.wait(self.sVEC, rc)
            A.activation(rnt, rnt, AF.Sqrt)
            sq_a = A.inc(self.sACT)
            V.wait(self.sACT, sq_a)
            V.tensor_scalar(rnt, rnt, 1e-6, TWOPI, ALU.max, ALU.mult)
            V.inc(self.sVEC)
            V.reciprocal(rnt, rnt)
            rn_m = V.inc(self.sVEC)

            wo_ready = []
            for qs in range(4):
                ptm3 = self.g1[:, qs * 1536:(qs + 1) * 1536].rearrange(
                    "p (h f) -> p h f", f=FE)
                rnv = self.g14[:, qs * 24:qs * 24 + 24].unsqueeze(2).broadcast_to(
                    [128, 24, FE])
                u = self.g2[:, 4096:5632]
                u3 = u.rearrange("p (h f) -> p h f", f=FE)
                w = self.g2[:, 5632:7168]
                w3 = w.rearrange("p (h f) -> p h f", f=FE)
                V.wait(self.sACT, ptm_last)
                V.wait(self.sVEC, rn_m)
                V.wait(self.sDG, self.sDG.n)
                V.tensor_tensor(u3, ptm3, rnv, ALU.mult); V.inc(self.sVEC)
                V.tensor_scalar(w, u, MAGIC, -MAGIC, ALU.add, ALU.add)
                V.inc(self.sVEC)
                V.tensor_tensor(w, u, w, ALU.subtract)
                wm1 = V.inc(self.sVEC)
                qp3 = self.g8.ap().rearrange("p (h f) -> p h f", f=128)
                A.wait(self.sVEC, wm1)
                A.activation(qp3[:, :, 0:FE], w3, AF.Sin, scale=TWOPI)
                as1 = A.inc(self.sACT)
                V.wait(self.sACT, as1)
                V.tensor_scalar(u, u, 0.25, None, ALU.add)
                V.inc(self.sVEC)
                V.tensor_scalar(w, u, MAGIC, -MAGIC, ALU.add, ALU.add)
                V.inc(self.sVEC)
                V.tensor_tensor(w, u, w, ALU.subtract)
                wm2 = V.inc(self.sVEC)
                A.wait(self.sVEC, wm2)
                A.activation(qp3[:, :, FE:128], w3, AF.Sin, scale=TWOPI)
                as2 = A.inc(self.sACT)

                Acols = self.g14[:, 128:256]
                V.wait(self.sACT, as2)
                if qs > 0:
                    V.wait(self.sW[2], wo_ready[-1])
                g15_3 = self.g15.ap().rearrange("p (h f) -> p h f", f=128)
                for hp in range(8):
                    kp = self.g8[:, (NH + hp) * 128:(NH + hp + 1) * 128]
                    kv = kp.unsqueeze(1).broadcast_to([128, NH, 128])
                    qwin = self.g8[:, 0:2048].rearrange("p (h f) -> p h f", f=128)
                    V.tensor_tensor(g15_3, qwin, kv, ALU.mult)
                    V.inc(self.sVEC)
                    V.tensor_reduce(Acols[:, hp * 16:(hp + 1) * 16], g15_3,
                                    axis=AX.X, op=ALU.add)
                    V.inc(self.sVEC)
                Ab = self.g9[:, 0:128]
                V.tensor_scalar(Ab, Acols, 2.0 / 128.0, None, ALU.mult)
                V.inc(self.sVEC)
                mix = self.g2[:, 0:2048]
                mix3 = mix.rearrange("p (h f) -> p h f", f=128)
                for hp in range(8):
                    Av = Ab[:, hp * 16:(hp + 1) * 16].unsqueeze(2).broadcast_to(
                        [128, NH, 128])
                    vv3 = self.g6[:, qs * 1024 + hp * 128:qs * 1024 + (hp + 1) * 128]\
                        .unsqueeze(1).broadcast_to([128, NH, 128])
                    if hp == 0:
                        V.tensor_tensor(mix3, Av, vv3, ALU.mult)
                        V.inc(self.sVEC)
                    else:
                        V.tensor_tensor(g15_3, Av, vv3, ALU.mult)
                        V.inc(self.sVEC)
                        V.tensor_tensor(mix, mix, self.g15[:], ALU.add)
                        V.inc(self.sVEC)
                mixb = self.g15[:]
                V.tensor_copy(mixb, mix)
                mxm = V.inc(self.sVEC)
                SY.wait(self.sVEC, mxm)
                SY.wait(self.sW[2], self.sW[2].n)
                SY.dma_start(
                    out=d["mixh"].rearrange("(h s) d -> s h d", h=NH)[
                        c * QC + qs * 128: c * QC + qs * 128 + 128, :, :],
                    in_=mixb.rearrange("p (h d) -> p h d", h=NH))
                mw = SY.inc(self.sW[2], 16)
                wo_ready.append(mw)
            P.barrier()
            self.plans.append((f"attna{c}", P))

    def attna_fin(self):
        import os
        kfin = int(os.environ.get("KFIN", "3"))
        d = self.d
        P = Plan(self)
        T, A, V = P.eng("tensor"), P.eng("scalar"), P.eng("vector")
        SY, GP = P.eng("sync"), P.eng("gpsimd")
        groups = [[2 * i, 2 * i + 1] for i in range(4)]
        GP.collective_compute("AllGather", ALU.bypass, replica_groups=groups,
                              ins=[d["mixh"].opt()], outs=[d["mixhg"].opt()])
        cc = GP.inc(self.sCC, 1)
        if kfin < 1:
            P.barrier()
            self.plans.append(("attna_fin", P))
            return
        mixview = d["mixhg"].rearrange("(n s) d -> n (s d)", s=16)
        tev = 0
        trdone = 0
        for c in range(NQC):
            cw = slice(c * QC, (c + 1) * QC)
            for tsub in range(4):
                j = c * 4 + tsub
                GP.wait(self.sCC, cc)
                if c > 0 or tsub > 0:
                    GP.wait(self.sPE, trdone)
                GP.indirect_dma_start(
                    out=self.g3[:, tsub * 2048:(tsub + 1) * 2048],
                    out_offset=None, in_=mixview,
                    in_offset=IndirectOffsetOnAxis(
                        self.c_idxscr[:, j:j + 1], axis=0),
                    oob_is_err=False)
                gm = GP.inc(self.sG[tsub], 16)
                for grp in range(2):
                    T.wait(self.sG[tsub], gm)
                    if not (c == 0 and tsub == 0 and grp == 0):
                        T.wait(self.sVEC, tev)
                    for k in range(grp * 8, grp * 8 + 8):
                        T.transpose(self.PT[:, (k % 8) * 128:(k % 8) * 128 + 128],
                                    self.g3[:, tsub * 2048 + k * 128:
                                            tsub * 2048 + (k + 1) * 128],
                                    self.c_ident[:])
                    tm = T.inc(self.sPE)
                    V.wait(self.sPE, tm)
                    for k in range(grp * 8, grp * 8 + 8):
                        V.tensor_copy(
                            self.g5[:, k * QC + tsub * 128:k * QC + tsub * 128 + 128],
                            self.PT[:, (k % 8) * 128:(k % 8) * 128 + 128])
                    tev = V.inc(self.sVEC)
                trdone = self.sPE.n
            if kfin < 2:
                continue
            wpm = [0] * NH
            aev_l = [0] * NH
            for ft in range(NH):
                if ft >= 2:
                    SY.wait(self.sPE, wpm[ft - 2])
                buf = self.g7[:, (ft % 2) * 2048:(ft % 2) * 2048 + 2048]
                if ft == 0:
                    SY.wait(self.sCAST2, self.castdone)
                SY.dma_start(
                    out=buf.rearrange("p (k f) -> p k f", k=NT),
                    in_=d["WoTb"].rearrange("(k p) f -> p k f", p=128)[:, :, ft * 128:(ft + 1) * 128])
                wl = SY.inc(self.sW[ft % 2], 16)
                bank = self.P6[ft % 2]
                T.wait(self.sW[ft % 2], wl)
                if ft == 0:
                    T.wait(self.sVEC, tev)
                if ft >= 2:
                    T.wait(self.sACT, aev_l[ft - 2])
                for k in range(NT):
                    T.matmul(bank[:], buf[:, k * 128:(k + 1) * 128],
                             self.g5[:, k * QC:(k + 1) * QC],
                             start=(k == 0), stop=False)
                T.matmul(bank[:], self.c_bo[0:1, ft * 128:(ft + 1) * 128],
                         self.c_o512[:], start=False, stop=True)
                wpm[ft] = T.inc(self.sPE)
                A.wait(self.sPE, wpm[ft])
                if ft == 0:
                    A.wait(self.sDG, self.sDG.n)
                A.activation(self.g2[:, ft * QC:(ft + 1) * QC], bank[:], AF.Copy)
                aev_l[ft] = A.inc(self.sACT)
            GP.wait(self.sACT, aev_l[NH - 1])
            GP.wait(self.sDS, 16)
            GP.dma_start(out=self.out_r[:, :, cw],
                         in_=self.g2[:, 0:8192].rearrange("p (k q) -> p k q", k=NT),
                         accum_op=ALU.add)
            GP.inc(self.sDG, 16)
            P.barrier()
        self.plans.append(("attna_fin", P))

    def ffn(self, tag):
        d = self.d
        for c in range(NQC):
            P = Plan(self)
            T, V, A = P.eng("tensor"), P.eng("vector"), P.eng("scalar")
            SY, GP = P.eng("sync"), P.eng("gpsimd")
            xl = self.ln_chunk(P, c)
            xlast = xl[NT - 1]
            agrows = 0
            tev = 0
            for qs in range(4):
                for grp in range(2):
                    if qs == 0 and grp == 0:
                        T.wait(self.sVEC, xlast)
                    else:
                        T.wait(self.sVEC, tev)
                    for k in range(grp * 8, grp * 8 + 8):
                        T.transpose(self.PT[:, (k % 8) * 128:(k % 8) * 128 + 128],
                                    self.g4[:, k * QC + qs * 128:k * QC + qs * 128 + 128],
                                    self.c_ident[:])
                    tm = T.inc(self.sPE)
                    V.wait(self.sPE, tm)
                    for k in range(grp * 8, grp * 8 + 8):
                        V.tensor_copy(self.g3[:, qs * 2048 + k * 128:qs * 2048 + (k + 1) * 128],
                                      self.PT[:, (k % 8) * 128:(k % 8) * 128 + 128])
                    tev = V.inc(self.sVEC)
                SY.wait(self.sVEC, tev)
                SY.wait(self.sW[3], self.sW[3].n)
                SY.dma_start(out=d["ag1in"][c * QC + qs * 128:c * QC + qs * 128 + 128, :],
                             in_=self.g3[:, qs * 2048:(qs + 1) * 2048])
                agrows = SY.inc(self.sW[3], 16)
            gl = 0
            for qs in range(4):
                for k in range(NT):
                    if qs == 0 and k == 0:
                        T.wait(self.sACT, self.sACT.n)
                    T.matmul(self.P6[2][:, qs * 8:(qs + 1) * 8],
                             self.g4[:, k * QC + qs * 128:k * QC + qs * 128 + 128],
                             self.c_gT[:, k * 8:(k + 1) * 8],
                             start=(k == 0), stop=(k == NT - 1))
            gl = T.inc(self.sPE)
            A.wait(self.sPE, gl)
            A.activation(self.g10[:, c * 32:(c + 1) * 32], self.P6[2][:, 0:32],
                         AF.Copy)
            A.inc(self.sACT)
            GP.wait(self.sW[3], agrows)
            GP.collective_compute(
                "AllGather", ALU.bypass, replica_groups=[list(range(NCORES))],
                ins=[d["ag1in"][c * QC:(c + 1) * QC, :].opt()],
                outs=[d["ag1out"][c * NCORES * QC:(c + 1) * NCORES * QC, :].opt()])
            GP.inc(self.sCC, 1)
            if c == NQC - 1:
                self.cag1 = self.sCC.n
            P.barrier()
            self.plans.append((f"ffnln{tag}_{c}", P))

        P = Plan(self)
        T, V, A = P.eng("tensor"), P.eng("vector"), P.eng("scalar")
        SY, GP = P.eng("sync"), P.eng("gpsimd")
        V.memset(self.g11[:, 0:168], 0.0)
        zv = V.inc(self.sVEC)
        SY.wait(self.sVEC, zv)
        SY.dma_start(out=d["fragi"].rearrange("(p f) one -> p (f one)", f=168),
                     in_=self.g11[:, 0:168])
        SY.inc(self.sD, 16)
        SY.dma_start(out=d["fragw"].rearrange("(p f) one -> p (f one)", f=168),
                     in_=self.g11[:, 0:168])
        zd = SY.inc(self.sD, 16)

        V.wait(self.sD, zd)
        cnt_ms = 0
        for j in range(NT):
            l = self.g10[:, j * 8:(j + 1) * 8]
            t0 = self.g11[:, 0:8]
            t1 = self.g11[:, 8:16]
            t2 = self.g11[:, 16:24]
            mcol = self.g11[:, 24:25]
            scol = self.g11[:, 25:26]
            V.reduce_max(mcol, l, axis=AX.X)
            V.tensor_scalar(mcol, mcol, -1.0, None, ALU.mult)
            vm = V.inc(self.sVEC)
            A.wait(self.sVEC, vm)
            A.activation(t0, l, AF.Exp, bias=mcol)
            ae = A.inc(self.sACT)
            V.wait(self.sACT, ae)
            V.reduce_sum(scol, t0, axis=AX.X)
            V.reciprocal(self.g14[:, j:j + 1], scol)
            V.tensor_scalar(t0, l, mcol, None, ALU.add)
            V.tensor_scalar(t0, t0, -0.0, None, ALU.is_ge)
            V.tensor_copy(t1, t0)
            V.tensor_tensor(t1[:, 1:8], t0[:, 1:8], t0[:, 0:7], ALU.add)
            V.tensor_copy(t2, t1)
            V.tensor_tensor(t2[:, 2:8], t1[:, 2:8], t1[:, 0:6], ALU.add)
            V.tensor_copy(t1, t2)
            V.tensor_tensor(t1[:, 4:8], t2[:, 4:8], t2[:, 0:4], ALU.add)
            V.tensor_scalar(t1, t1, 1.0, None, ALU.is_equal)
            V.tensor_tensor(t0, t0, t1, ALU.mult)
            V.tensor_copy(self.g8[:, j * 8:(j + 1) * 8], t0)
            fb = V.inc(self.sVEC)
            T.wait(self.sVEC, fb)
            T.matmul(self.P6[5][0:8, j:j + 1], self.g8[:, j * 8:(j + 1) * 8],
                     self.c_ones[:], start=True, stop=True)
            cnt_ms = T.inc(self.sPE)
        A.wait(self.sPE, cnt_ms)
        A.activation(self.g11[0:8, 512:528], self.P6[5][0:8, 0:16], AF.Copy)
        cev = A.inc(self.sACT)
        SY.wait(self.sACT, cev)
        SY.dma_start(out=d["cntin"].rearrange("j e -> e j"),
                     in_=self.g11[0:8, 512:528],
                     allow_slow_non_contiguous=True)
        cnd = SY.inc(self.sD, 16)
        GP.wait(self.sD, cnd)
        GP.collective_compute("AllGather", ALU.bypass,
                              replica_groups=[list(range(NCORES))],
                              ins=[d["cntin"].opt()], outs=[d["cntout"].opt()])
        GP.inc(self.sCC, 1)
        ccc = self.sCC.n
        cag1 = self.cag1

        SY.wait(self.sCC, ccc)
        SY.dma_start(out=self.g11[:, 768:776], in_=d["cntout"])
        cl = SY.inc(self.sD, 16)
        V.wait(self.sD, cl)
        V.tensor_copy(self.g13[:, 0:8], self.g11[:, 768:776])
        cbf = V.inc(self.sVEC)

        rk_ev = 0
        rkread = 0
        for j in range(NT):
            bank = self.P6[j % 2]
            T.wait(self.sVEC, cbf)
            if j >= 2:
                T.wait(self.sACT, rk_ev)
            T.matmul(bank[:, 0:8], self.c_tri[:], self.g8[:, j * 8:(j + 1) * 8],
                     start=True, stop=False)
            T.matmul(bank[:, 0:8], self.c_trimyr[:, j * 128:(j + 1) * 128],
                     self.g13[:, 0:8], start=False, stop=False)
            T.matmul(bank[:, 0:8], self.c_onesr[:], self.c_nego[0:1, 0:8],
                     start=False, stop=True)
            rkm = T.inc(self.sPE)
            A.wait(self.sPE, rkm)
            if rkread:
                A.wait(self.sVEC, rkread)
            A.activation(self.g11[:, 32:40], bank[:, 0:8], AF.Copy)
            rk_ev = A.inc(self.sACT)
            rk = self.g11[:, 32:40]
            pr = self.g11[:, 40:48]
            rsel = self.g11[:, 48:49]
            esel = self.g11[:, 49:50]
            keep = self.g11[:, 50:51]
            wtok = self.g11[:, 51:52]
            tmp = self.g11[:, 52:53]
            tmp2 = self.g11[:, 53:54]
            tmp3 = self.g11[:, 54:55]
            V.wait(self.sACT, rk_ev)
            V.tensor_tensor(pr, self.g8[:, j * 8:(j + 1) * 8], rk, ALU.mult)
            V.reduce_sum(rsel, pr, axis=AX.X)
            V.tensor_tensor(pr, self.g8[:, j * 8:(j + 1) * 8], self.c_iota[:],
                            ALU.mult)
            V.reduce_sum(esel, pr, axis=AX.X)
            rkread = V.inc(self.sVEC)
            V.tensor_scalar(keep, rsel, float(CAP) - 0.5, None, ALU.is_lt)
            V.tensor_tensor(wtok, self.g14[:, j:j + 1], keep, ALU.mult)
            V.tensor_scalar(tmp, esel, float(CAPP), None, ALU.mult)
            V.tensor_tensor(tmp2, tmp, rsel, ALU.add)
            V.tensor_scalar(tmp3, tmp, 2560.0, None, ALU.add)
            V.tensor_tensor(tmp, tmp2, tmp3, ALU.subtract)
            V.tensor_scalar(tmp, tmp, keep, None, ALU.mult)
            V.tensor_tensor(tmp, tmp, tmp3, ALU.add)
            V.tensor_copy(self.c_sci[:, j % 2:j % 2 + 1], tmp)
            sci = V.inc(self.sVEC)
            GP.wait(self.sVEC, sci)
            GP.indirect_dma_start(
                out=d["fragi"],
                out_offset=IndirectOffsetOnAxis(self.c_sci[:, j % 2:j % 2 + 1], axis=0),
                in_=self.c_gidx[:, j:j + 1], in_offset=None, oob_is_err=False)
            GP.inc(self.sSC, 16)
            GP.indirect_dma_start(
                out=d["fragw"],
                out_offset=IndirectOffsetOnAxis(self.c_sci[:, j % 2:j % 2 + 1], axis=0),
                in_=wtok, in_offset=None, oob_is_err=False)
            sc2 = GP.inc(self.sSC, 16)
            V.wait(self.sSC, sc2)
            V.tensor_scalar(tmp, rsel, -10.0, 1.0 / 21.0, ALU.add, ALU.mult)
            V.tensor_scalar(tmp, tmp, MAGIC, -MAGIC, ALU.add, ALU.add)
            V.scalar_tensor_tensor(tmp2, tmp, -21.0, rsel, ALU.mult, ALU.add)
            V.tensor_scalar(tmp3, tmp2, 10.5, None, ALU.is_gt)
            V.scalar_tensor_tensor(tmp, tmp3, -1408.0, tmp, ALU.mult, ALU.add)
            V.scalar_tensor_tensor(tmp, tmp2, 128.0, tmp, ALU.mult, ALU.add)
            V.scalar_tensor_tensor(tmp, tmp3, 11264.0 + 1408.0 * 0, tmp,
                                   ALU.mult, ALU.add)
            V.scalar_tensor_tensor(tmp, esel, 1408.0, tmp, ALU.mult, ALU.add)
            V.tensor_tensor(tmp2, tmp, self.c_constp[:, 3:4], ALU.subtract)
            V.tensor_scalar(tmp2, tmp2, keep, None, ALU.mult)
            V.tensor_tensor(tmp2, tmp2, self.c_constp[:, 3:4], ALU.add)
            V.tensor_copy(self.c_cri[:, j:j + 1], tmp2)
            V.inc(self.sVEC)
        GP.wait(self.sVEC, self.sVEC.n)
        GP.wait(self.sSC, self.sSC.n)
        GP.collective_compute("ReduceScatter", ALU.add,
                              replica_groups=[list(range(NCORES))],
                              ins=[d["fragi"].opt()], outs=[d["fragio"].opt()])
        GP.inc(self.sCC, 1)
        GP.collective_compute("ReduceScatter", ALU.add,
                              replica_groups=[list(range(NCORES))],
                              ins=[d["fragw"].opt()], outs=[d["fragwo"].opt()])
        GP.inc(self.sCC, 1)
        crs = self.sCC.n
        P.barrier()
        self.plans.append((f"route{tag}", P))

        P = Plan(self)
        T, V, A = P.eng("tensor"), P.eng("vector"), P.eng("scalar")
        SY, GP = P.eng("sync"), P.eng("gpsimd")
        SY.wait(self.sCC, crs)
        SY.dma_start(out=self.g11[:, 0:21],
                     in_=d["fragio"].rearrange("(p f) one -> p (f one)", f=21))
        SY.inc(self.sD, 16)
        SY.dma_start(out=self.g11[:, 32:53],
                     in_=d["fragwo"].rearrange("(p f) one -> p (f one)", f=21))
        sl = SY.inc(self.sD, 16)
        V.wait(self.sD, sl)
        V.tensor_scalar(self.g11[:, 0:21], self.g11[:, 0:21],
                        float(NCORES * TLOC - 1), None, ALU.min)
        V.tensor_copy(self.c_sci[:, 2:23], self.g11[:, 0:21])
        siv = V.inc(self.sVEC)

        SY.wait(self.sCAST2, self.castdone)
        SY.wait(self.sCC, self.sCC.n)
        GP.wait(self.sCC, self.sCC.n)
        fgroups = [list(range(0, 4)), list(range(4, 8)), list(range(8, 11)),
                   list(range(11, 15)), list(range(15, 19)), list(range(19, 21))]
        bankfree = [0, 0, 0, 0]
        slotfree = [0, 0, 0, 0]
        tr_ms = 0
        otm = 0
        g3free = 0
        ag2flags = []
        for gi, fg in enumerate(fgroups):
            nw = len(fg) * 128
            gath_ms = [0, 0, 0, 0]
            for si, f in enumerate(fg):
                GP.wait(self.sVEC, siv)
                GP.wait(self.sCC, cag1)
                if gi > 0:
                    GP.wait(self.sPE, g3free)
                GP.indirect_dma_start(
                    out=self.g3[:, si * 2048:(si + 1) * 2048], out_offset=None,
                    in_=d["ag1out"],
                    in_offset=IndirectOffsetOnAxis(self.c_sci[:, 2 + f:3 + f], axis=0),
                    oob_is_err=False)
                gath_ms[si] = GP.inc(self.sG[si], 16)
            prev_t = otm
            for si in range(len(fg)):
                for grp in range(2):
                    T.wait(self.sG[si % 4], gath_ms[si])
                    if prev_t:
                        T.wait(self.sVEC, prev_t)
                    for k in range(grp * 8, grp * 8 + 8):
                        T.transpose(self.PT[:, (k % 8) * 128:(k % 8) * 128 + 128],
                                    self.g3[:, si * 2048 + k * 128:si * 2048 + (k + 1) * 128],
                                    self.c_ident[:])
                    tm = T.inc(self.sPE)
                    V.wait(self.sPE, tm)
                    for k in range(grp * 8, grp * 8 + 8):
                        V.tensor_copy(self.g5[:, k * nw + si * 128:k * nw + si * 128 + 128],
                                      self.PT[:, (k % 8) * 128:(k % 8) * 128 + 128])
                    prev_t = V.inc(self.sVEC)
            tr_ms = self.sPE.n
            xfm_ok = prev_t
            hev = [0] * 16
            for it in range(16):
                SY.wait(self.sPE, slotfree[it % 2])
                buf = self.g7[:, (it % 2) * 2048:(it % 2) * 2048 + 2048]
                SY.dma_start(
                    out=buf.rearrange("p (k f) -> p k f", k=NT),
                    in_=d["W1b"].rearrange("(k p) f -> p k f", p=128)[:, :, it * 128:(it + 1) * 128])
                wl = SY.inc(self.sW[it % 2], 16)
                bank = self.P6[it % 2]
                T.wait(self.sW[it % 2], wl)
                if it == 0:
                    T.wait(self.sVEC, xfm_ok)
                T.wait(self.sACT, bankfree[it % 2])
                for k in range(NT):
                    T.matmul(bank[:, 0:nw],
                             buf.rearrange("p (k f) -> p k f", k=NT)[:, k, :],
                             self.g5[:, k * nw:(k + 1) * nw],
                             start=(k == 0), stop=(k == NT - 1))
                hm = T.inc(self.sPE)
                slotfree[it % 2] = hm
                A.wait(self.sPE, hm)
                A.activation(self.g4[:, it * nw:(it + 1) * nw], bank[:, 0:nw],
                             AF.Gelu)
                hev[it] = A.inc(self.sACT)
                bankfree[it % 2] = hev[it]
            oev = [0] * 16
            for ht in range(16):
                SY.wait(self.sPE, slotfree[2 + ht % 2])
                buf = self.g6[:, (ht % 2) * 2048:(ht % 2) * 2048 + 2048]
                SY.dma_start(
                    out=buf.rearrange("p (k f) -> p k f", k=NT),
                    in_=d["W2b"].rearrange("(k p) f -> p k f", p=128)[:, :, ht * 128:(ht + 1) * 128])
                wl = SY.inc(self.sW[2 + ht % 2], 16)
                bank = self.P6[2 + ht % 2]
                T.wait(self.sW[2 + ht % 2], wl)
                if ht == 0:
                    T.wait(self.sACT, hev[15])
                    T.wait(self.sPE, tr_ms)
                T.wait(self.sACT, bankfree[2 + ht % 2])
                for k in range(NT):
                    T.matmul(bank[:, 0:nw],
                             buf.rearrange("p (k f) -> p k f", k=NT)[:, k, :],
                             self.g4[:, k * nw:(k + 1) * nw],
                             start=(k == 0), stop=(k == NT - 1))
                om = T.inc(self.sPE)
                slotfree[2 + ht % 2] = om
                A.wait(self.sPE, om)
                A.activation(self.g3[:, ht * nw:(ht + 1) * nw], bank[:, 0:nw],
                             AF.Copy)
                oev[ht] = A.inc(self.sACT)
                bankfree[2 + ht % 2] = oev[ht]
            for si, f in enumerate(fg):
                for grp in range(2):
                    T.wait(self.sACT, oev[15])
                    if otm:
                        T.wait(self.sVEC, otm)
                    for ht in range(grp * 8, grp * 8 + 8):
                        T.transpose(self.PT[:, (ht % 8) * 128:(ht % 8) * 128 + 128],
                                    self.g3[:, ht * nw + si * 128:ht * nw + si * 128 + 128],
                                    self.c_ident[:])
                    tm = T.inc(self.sPE)
                    V.wait(self.sPE, tm)
                    if grp == 0:
                        V.wait(self.sW[3], self.sW[3].n)
                    for ht in range(grp * 8, grp * 8 + 8):
                        V.tensor_copy(self.g15[:, ht * 128:(ht + 1) * 128],
                                      self.PT[:, (ht % 8) * 128:(ht % 8) * 128 + 128])
                    otm = V.inc(self.sVEC)
                V.tensor_scalar(self.g15[:], self.g15[:],
                                self.g11[:, 32 + f:33 + f], None, ALU.mult)
                sc = V.inc(self.sVEC)
                SY.wait(self.sVEC, sc)
                SY.dma_start(out=d["ag2in"][f * 128:(f + 1) * 128, :],
                             in_=self.g15[:])
                SY.inc(self.sW[3], 16)
            g3free = self.sPE.n
            if gi == 2:
                GP.wait(self.sW[3], self.sW[3].n)
                GP.collective_compute(
                    "AllGather", ALU.bypass, replica_groups=[list(range(NCORES))],
                    ins=[d["ag2in"][0:1408, :].opt()],
                    outs=[d["ag2out"][0:8 * 1408, :].opt()])
                GP.inc(self.sCC, 1)
                ag2flags.append(self.sCC.n)
            if gi == 5:
                GP.wait(self.sW[3], self.sW[3].n)
                GP.collective_compute(
                    "AllGather", ALU.bypass, replica_groups=[list(range(NCORES))],
                    ins=[d["ag2in"][1408:AG2R, :].opt()],
                    outs=[d["ag2out"][8 * 1408:2 * 8 * 1408, :].opt()])
                GP.inc(self.sCC, 1)
                ag2flags.append(self.sCC.n)
        P.barrier()
        self.plans.append((f"expert{tag}", P))

        P = Plan(self)
        T, V, A = P.eng("tensor"), P.eng("vector"), P.eng("scalar")
        SY, GP = P.eng("sync"), P.eng("gpsimd")
        gme = [0, 0, 0, 0]
        tev = 0
        trprev = 0
        for c in range(NQC):
            cw = slice(c * QC, (c + 1) * QC)
            for tsub in range(4):
                j = c * 4 + tsub
                GP.wait(self.sCC, ag2flags[1])
                if c > 0:
                    GP.wait(self.sPE, trprev)
                GP.indirect_dma_start(
                    out=self.g3[:, tsub * 2048:(tsub + 1) * 2048], out_offset=None,
                    in_=d["ag2out"],
                    in_offset=IndirectOffsetOnAxis(self.c_cri[:, j:j + 1], axis=0),
                    oob_is_err=False)
                gme[tsub] = GP.inc(self.sG[tsub], 16)
            for tsub in range(4):
                for grp in range(2):
                    T.wait(self.sG[tsub], gme[tsub])
                    if tev:
                        T.wait(self.sVEC, tev)
                    for k in range(grp * 8, grp * 8 + 8):
                        T.transpose(self.PT[:, (k % 8) * 128:(k % 8) * 128 + 128],
                                    self.g3[:, tsub * 2048 + k * 128:tsub * 2048 + (k + 1) * 128],
                                    self.c_ident[:])
                    tm = T.inc(self.sPE)
                    V.wait(self.sPE, tm)
                    for k in range(grp * 8, grp * 8 + 8):
                        V.tensor_copy(
                            self.g5[:, k * QC + tsub * 128:k * QC + tsub * 128 + 128],
                            self.PT[:, (k % 8) * 128:(k % 8) * 128 + 128])
                    tev = V.inc(self.sVEC)
            trprev = self.sPE.n
            V.wait(self.sDG, self.sDG.n)
            for k in range(NT):
                V.tensor_copy(self.g2[:, k * QC:(k + 1) * QC],
                              self.g5[:, k * QC:(k + 1) * QC])
            cv = V.inc(self.sVEC)
            GP.wait(self.sVEC, cv)
            GP.dma_start(out=self.out_r[:, :, cw],
                         in_=self.g2[:, 0:8192].rearrange("p (k q) -> p k q", k=NT),
                         accum_op=ALU.add)
            GP.inc(self.sDG, 16)
            P.barrier()
        self.plans.append((f"combine{tag}", P))

    def attn_b(self):
        d = self.d
        for c in range(NQC):
            P = Plan(self)
            T, V, A = P.eng("tensor"), P.eng("vector"), P.eng("scalar")
            SY, GP = P.eng("sync"), P.eng("gpsimd")
            cw = slice(c * QC, (c + 1) * QC)
            xl = self.ln_chunk(P, c)
            xlast = xl[NT - 1]
            if c == 0:
                SY.wait(self.sCAST2, self.castdone)
            wld = [0, 0]
            evms = [0, 0]
            consumed = [0, 0]
            for i in range(32):
                isk = i < 16
                h = i % 16
                fcol = (2048 + h * 128) if isk else (h * 128)
                if i >= 2:
                    SY.wait(self.sPE, wld[i % 2])
                buf = self.g7[:, (i % 2) * 2048:(i % 2) * 2048 + 2048]
                SY.dma_start(
                    out=buf.rearrange("p (k f) -> p k f", k=NT),
                    in_=d["Wqkvb"].rearrange("(k p) f -> p k f", p=128)[:, :, fcol:fcol + 128])
                wl = SY.inc(self.sW[i % 2], 16)
                bank = self.P6[i % 2]
                T.wait(self.sW[i % 2], wl)
                if i == 0:
                    T.wait(self.sVEC, xlast)
                if i >= 2:
                    T.wait(self.sACT, evms[i % 2])
                for k in range(NT):
                    T.matmul(bank[:], buf[:, k * 128:(k + 1) * 128],
                             self.g4[:, k * QC:(k + 1) * QC],
                             start=(k == 0), stop=False)
                T.matmul(bank[:], self.c_bqkv[0:1, fcol:fcol + 128],
                         self.c_o512[:], start=False, stop=True)
                pm = T.inc(self.sPE)
                wld[i % 2] = pm
                ev = self.g9[:, (i % 2) * QC:(i % 2) * QC + QC]
                A.wait(self.sPE, pm)
                if i >= 2:
                    A.wait(self.sVEC, consumed[i % 2])
                    A.wait(self.sW[2], self.sW[2].n)
                A.activation(ev, bank[:], AF.Copy)
                evms[i % 2] = A.inc(self.sACT)
                sq = (self.g5 if isk else self.g3)[:, h * QC:(h + 1) * QC]
                V.wait(self.sACT, evms[i % 2])
                V.tensor_tensor(sq, ev, ev, ALU.mult)
                consumed[i % 2] = V.inc(self.sVEC)
                SY.wait(self.sACT, evms[i % 2])
                SY.dma_start(
                    out=(d["kvK"] if isk else d["qdram"])[h * 128:(h + 1) * 128, cw],
                    in_=ev)
                SY.inc(self.sW[2], 16)
            sqv = self.sVEC.n
            T.wait(self.sVEC, sqv)
            for h2 in range(16):
                T.matmul(self.P6[4][0:16, :], self.c_c255[:, 127 - h2:143 - h2],
                         self.g5[:, h2 * QC:(h2 + 1) * QC],
                         start=(h2 == 0), stop=(h2 == 15))
            T.inc(self.sPE)
            for h2 in range(16):
                T.matmul(self.P6[5][0:16, :], self.c_c255[:, 127 - h2:143 - h2],
                         self.g3[:, h2 * QC:(h2 + 1) * QC],
                         start=(h2 == 0), stop=(h2 == 15))
            T.inc(self.sPE)
            nmm = self.sPE.n
            A.wait(self.sPE, nmm)
            A.activation(self.g11[0:16, 512:1024], self.P6[4][0:16, :], AF.Copy)
            A.activation(self.g11[0:16, 0:512], self.P6[5][0:16, :], AF.Copy)
            nev = A.inc(self.sACT)
            V.wait(self.sACT, nev)
            V.reduce_max(self.g14[0:16, 302:303], self.g11[0:16, 512:1024], axis=AX.X)
            V.reduce_max(self.g14[0:16, 303:304], self.g11[0:16, 0:512], axis=AX.X)
            if c == 0:
                V.tensor_copy(self.g14[0:16, 301:302], self.g14[0:16, 302:303])
                V.tensor_copy(self.g14[0:16, 300:301], self.g14[0:16, 303:304])
            else:
                V.tensor_tensor(self.g14[0:16, 301:302], self.g14[0:16, 301:302],
                                self.g14[0:16, 302:303], ALU.max)
                V.tensor_tensor(self.g14[0:16, 300:301], self.g14[0:16, 300:301],
                                self.g14[0:16, 303:304], ALU.max)
            V.inc(self.sVEC)
            vth = [0, 0, 0, 0]
            for nch in range(4):
                vb = 0
                for k in range(NT):
                    idx = nch * NT + k
                    if idx >= 4:
                        SY.wait(self.sPE, vth[idx % 4])
                    else:
                        SY.wait(self.sPE, wld[0])
                        SY.wait(self.sPE, wld[1])
                    SY.dma_start(out=self.g7[:, (idx % 4) * 512:(idx % 4) * 512 + 512],
                                 in_=d["Wqkvb"][k * 128:(k + 1) * 128,
                                                4096 + nch * 512:4096 + (nch + 1) * 512])
                    vl = SY.inc(self.sW[idx % 4], 16)
                    T.wait(self.sW[idx % 4], vl)
                    if idx == 0:
                        T.wait(self.sACT, evms[0])
                        T.wait(self.sACT, evms[1])
                    if k == 0 and nch > 0:
                        T.wait(self.sVEC, vpe)
                    for qs in range(4):
                        T.matmul(self.P6[qs][:],
                                 self.g4[:, k * QC + qs * 128:k * QC + qs * 128 + 128],
                                 self.g7[:, (idx % 4) * 512:(idx % 4) * 512 + 512],
                                 start=(k == 0), stop=False)
                    vth[idx % 4] = T.inc(self.sPE)
                for qs in range(4):
                    T.matmul(self.P6[qs][:], self.c_onesr[:],
                             self.c_bvb[0:1, nch * 512:(nch + 1) * 512],
                             start=False, stop=True)
                vm = T.inc(self.sPE)
                V.wait(self.sPE, vm)
                for qs in range(4):
                    V.tensor_copy(self.g3[:, qs * 2048 + nch * 512:qs * 2048 + (nch + 1) * 512],
                                  self.P6[qs][:])
                vpe = V.inc(self.sVEC)
            for qs in range(4):
                SY.wait(self.sVEC, vpe)
                SY.wait(self.sW[2], self.sW[2].n)
                SY.dma_start(out=d["kvV"][c * QC + qs * 128:c * QC + qs * 128 + 128, :],
                             in_=self.g3[:, qs * 2048:(qs + 1) * 2048])
                SY.inc(self.sW[2], 16)
            P.barrier()
            self.plans.append((f"bqkv{c}", P))

        P = Plan(self)
        T, V, A = P.eng("tensor"), P.eng("vector"), P.eng("scalar")
        SY, GP = P.eng("sync"), P.eng("gpsimd")
        SY.dma_start(out=d["mkq"][0:16, :], in_=self.g14[0:16, 301:302])
        SY.inc(self.sD, 16)
        SY.dma_start(out=d["mkq"][16:32, :], in_=self.g14[0:16, 300:301])
        mq = SY.inc(self.sD, 16)
        groups = [[2 * i, 2 * i + 1] for i in range(4)]
        GP.wait(self.sD, mq)
        GP.wait(self.sW[2], self.sW[2].n)
        GP.collective_compute("AllGather", ALU.bypass, replica_groups=groups,
                              ins=[d["mkq"].opt()], outs=[d["mkqg"].opt()])
        GP.inc(self.sCC, 1)
        GP.collective_compute("AllGather", ALU.bypass, replica_groups=groups,
                              ins=[d["kvK"].opt()], outs=[d["kvKg"].opt()])
        GP.inc(self.sCC, 1)
        GP.collective_compute("AllGather", ALU.bypass, replica_groups=groups,
                              ins=[d["kvV"].opt()], outs=[d["kvVg"].opt()])
        kvc = GP.inc(self.sCC, 1)
        SY.wait(self.sCC, kvc)
        SY.dma_start(out=self.g11[0:16, 0:1], in_=d["mkqg"][16:32, :])
        SY.inc(self.sD, 16)
        SY.dma_start(out=self.g11[0:16, 1:2], in_=d["mkqg"][48:64, :])
        mk2 = SY.inc(self.sD, 16)
        V.wait(self.sD, mk2)
        V.tensor_tensor(self.g11[0:16, 2:3], self.g11[0:16, 0:1],
                        self.g11[0:16, 1:2], ALU.max)
        V.tensor_tensor(self.g11[0:16, 2:3], self.g11[0:16, 2:3],
                        self.g14[0:16, 301:302], ALU.mult)
        bb = V.inc(self.sVEC)
        A.wait(self.sVEC, bb)
        A.activation(self.g11[0:16, 3:4], self.g11[0:16, 2:3], AF.Sqrt)
        bsq = A.inc(self.sACT)
        V.wait(self.sACT, bsq)
        V.tensor_scalar(self.g11[0:16, 4:5], self.g11[0:16, 3:4], -SCALE_B, None,
                        ALU.mult)
        bneg = V.inc(self.sVEC)
        SY.wait(self.sVEC, bneg)
        SY.dma_start(out=self.g12[0:1, 1024:1040],
                     in_=self.g11[0:16, 4:5],
                     allow_slow_non_contiguous=True)
        brow = SY.inc(self.sD, 16)
        V.wait(self.sD, brow)
        V.tensor_copy(self.g13[0:1, 0:16], self.g12[0:1, 1024:1040])
        brb = V.inc(self.sVEC)
        T.wait(self.sVEC, brb)
        T.matmul(self.P6[3][:, 0:16], self.c_onesr[:], self.g13[0:1, 0:16],
                 start=True, stop=True)
        bm = T.inc(self.sPE)
        A.wait(self.sPE, bm)
        A.activation(self.g14[:, 304:320], self.P6[3][:, 0:16], AF.Copy)
        self.bnegms = A.inc(self.sACT)
        P.barrier()
        self.plans.append(("bkv", P))

        P = Plan(self)
        T, V, A = P.eng("tensor"), P.eng("vector"), P.eng("scalar")
        SY, GP = P.eng("sync"), P.eng("gpsimd")
        SY.wait(self.sCC, self.sCC.n)
        scol = lambda g: (g % 2) * 2048 + (g // 2) * 128
        hfree = [0, 0]
        yst = 0
        ynm_prev = 0
        kqlh = [None, None]
        def loads(h):
            hb = h % 2
            kbuf = self.g5 if hb == 0 else self.g3
            sK = self.sW[0] if hb == 0 else self.sH[0]
            sV = self.sW[1] if hb == 0 else self.sH[1]
            sQ = self.sW[2] if hb == 0 else self.sH[2]
            SY.wait(self.sPE, hfree[hb])
            for r in range(2):
                SY.dma_start(out=kbuf[:, r * 2048:(r + 1) * 2048],
                             in_=d["kvKg"][r * 2048 + h * 128:r * 2048 + (h + 1) * 128, :])
                SY.inc(sK, 16)
            for g in range(32):
                SY.dma_start(
                    out=kbuf[:, 4096 + scol(g):4096 + scol(g) + 128],
                    in_=d["kvVg"][(g % 2) * 2048 + (g // 2) * 128:
                                  (g % 2) * 2048 + (g // 2) * 128 + 128,
                                  h * 128:(h + 1) * 128])
                SY.inc(sV, 16)
            SY.dma_start(out=self.g7[:, hb * 2048:hb * 2048 + 2048],
                         in_=d["qdram"][h * 128:(h + 1) * 128, :])
            SY.inc(sQ, 16)
            kqlh[hb] = ((sK, sK.n), (sV, sV.n), (sQ, sQ.n))
        loads(0)
        for h in range(NH):
            hb = h % 2
            kbuf = self.g5 if hb == 0 else self.g3
            qbuf = self.g7[:, hb * 2048:hb * 2048 + 2048]
            kql = kqlh[hb]
            if h + 1 < NH:
                loads(h + 1)
            bcol = self.g14[:, 304 + h:305 + h]
            for qc in range(4):
                qw = slice(qc * QC, (qc + 1) * QC)
                gmax = min(31, 8 * qc + 7)
                pexp = [0, 0]
                pvdone = [0, 0]
                mdone = [0, 0]
                for g in range(gmax + 2):
                    if g <= gmax:
                        jstart = max(0, g // 2 - 4 * qc)
                        qlo = jstart * 128
                        sb = self.P6[g % 2]
                        T.wait(*kql[0])
                        T.wait(*kql[2])
                        if h == 0 and qc == 0 and g == 0:
                            T.wait(self.sACT, self.bnegms)
                        if g >= 2:
                            T.wait(self.sACT, pexp[g % 2])
                        T.matmul(sb[:, qlo:512],
                                 kbuf[:, scol(g):scol(g) + 128],
                                 qbuf[:, qc * QC + qlo:(qc + 1) * QC],
                                 start=True, stop=True)
                        sm = T.inc(self.sPE)
                        pt = self.g9[:, (g % 2) * QC:(g % 2) * QC + QC]
                        A.wait(self.sPE, sm)
                        if g >= 2:
                            A.wait(self.sPE, pvdone[g % 2])
                        A.activation(pt[:, qlo:512], sb[:, qlo:512], AF.Exp,
                                     bias=bcol, scale=SCALE_B)
                        pexp[g % 2] = A.inc(self.sACT)
                        mdone[g % 2] = 0
                        if 0 <= g // 2 - 4 * qc < 4:
                            jloc = g // 2 - 4 * qc
                            V.wait(self.sACT, pexp[g % 2])
                            V.tensor_tensor(pt[:, jloc * 128:(jloc + 1) * 128],
                                            pt[:, jloc * 128:(jloc + 1) * 128],
                                            (self.c_m1 if g % 2 == 0 else self.c_m2)[:],
                                            ALU.mult)
                            mdone[g % 2] = V.inc(self.sVEC)
                    gp_ = g - 1
                    if 0 <= gp_ <= gmax:
                        jstart = max(0, gp_ // 2 - 4 * qc)
                        qlo = jstart * 128
                        pt = self.g9[:, (gp_ % 2) * QC:(gp_ % 2) * QC + QC]
                        T.wait(self.sACT, pexp[gp_ % 2])
                        if mdone[gp_ % 2]:
                            T.wait(self.sVEC, mdone[gp_ % 2])
                        T.wait(*kql[1])
                        if gp_ == 0 and ynm_prev:
                            T.wait(self.sVEC, ynm_prev)
                        T.matmul(self.P6[2][:, qlo:512],
                                 kbuf[:, 4096 + scol(gp_):4096 + scol(gp_) + 128],
                                 pt[:, qlo:512],
                                 start=(gp_ == 0), stop=(gp_ == gmax))
                        T.matmul(self.P7[0:1, qlo:512], self.c_ones[:],
                                 pt[:, qlo:512],
                                 start=(gp_ == 0), stop=(gp_ == gmax))
                        pvdone[gp_ % 2] = T.inc(self.sPE)
                pvm = pvdone[gmax % 2]
                A.wait(self.sPE, pvm)
                A.activation(self.g11[0:1, 512:1024], self.P7[0:1, :], AF.Copy)
                csv = A.inc(self.sACT)
                V.wait(self.sACT, csv)
                V.reciprocal(self.g11[0:1, 512:1024], self.g11[0:1, 512:1024])
                V.tensor_copy(self.g13[0:1, 16:528], self.g11[0:1, 512:1024])
                rcb = V.inc(self.sVEC)
                T.wait(self.sVEC, rcb)
                T.matmul(self.P6[3][:], self.c_onesr[:], self.g13[0:1, 16:528],
                         start=True, stop=True)
                bcm = T.inc(self.sPE)
                A.wait(self.sPE, bcm)
                A.activation(self.g11[:, 0:512], self.P6[3][:], AF.Copy)
                bce = A.inc(self.sACT)
                V.wait(self.sACT, bce)
                if yst:
                    V.wait(self.sW[3], yst)
                V.tensor_tensor(self.g15[:, 0:512], self.P6[2][:],
                                self.g11[:, 0:512], ALU.mult)
                ynm = V.inc(self.sVEC)
                ynm_prev = ynm
                SY.wait(self.sVEC, ynm)
                SY.dma_start(out=d["ydram"][h * 128:(h + 1) * 128, qw],
                             in_=self.g15[:, 0:512])
                yst = SY.inc(self.sW[3], 16)
            hfree[hb] = self.sPE.n
        P.barrier()
        self.plans.append(("batt", P))

        for c in range(NQC):
            P = Plan(self)
            T, V, A = P.eng("tensor"), P.eng("vector"), P.eng("scalar")
            SY, GP = P.eng("sync"), P.eng("gpsimd")
            cw = slice(c * QC, (c + 1) * QC)
            SY.dma_start(out=self.g4.ap().rearrange("p (k q) -> p k q", k=NT),
                         in_=d["ydram"].rearrange("(k p) q -> p k q", p=128)[:, :, cw])
            yl = SY.inc(self.sD, 16)
            wpm = [0, 0]
            aev = [0, 0]
            for ft in range(NH):
                if ft >= 2:
                    SY.wait(self.sPE, wpm[ft % 2])
                buf = self.g7[:, (ft % 2) * 2048:(ft % 2) * 2048 + 2048]
                SY.dma_start(
                    out=buf.rearrange("p (k f) -> p k f", k=NT),
                    in_=d["Woutb"].rearrange("(k p) f -> p k f", p=128)[:, :, ft * 128:(ft + 1) * 128])
                wl = SY.inc(self.sW[ft % 2], 16)
                bank = self.P6[ft % 2]
                T.wait(self.sW[ft % 2], wl)
                if ft < 2:
                    T.wait(self.sD, yl)
                if ft >= 2:
                    T.wait(self.sACT, aev[ft % 2])
                for k in range(NT):
                    T.matmul(bank[:], buf[:, k * 128:(k + 1) * 128],
                             self.g4[:, k * QC:(k + 1) * QC],
                             start=(k == 0), stop=False)
                T.matmul(bank[:], self.c_bout[0:1, ft * 128:(ft + 1) * 128],
                         self.c_o512[:], start=False, stop=True)
                wpm[ft % 2] = T.inc(self.sPE)
                A.wait(self.sPE, wpm[ft % 2])
                if ft == 0:
                    A.wait(self.sDG, self.sDG.n)
                A.activation(self.g2[:, ft * QC:(ft + 1) * QC], bank[:], AF.Copy)
                aev[ft % 2] = A.inc(self.sACT)
            GP.wait(self.sACT, aev[(NH - 1) % 2])
            GP.wait(self.sACT, aev[(NH - 2) % 2])
            GP.dma_start(out=self.out_r[:, :, cw],
                         in_=self.g2[:, 0:8192].rearrange("p (k q) -> p k q", k=NT),
                         accum_op=ALU.add)
            GP.inc(self.sDG, 16)
            P.barrier()
            self.plans.append((f"bwout{c}", P))

    def finish(self):
        P = Plan(self)
        P.barrier()
        for e in ENGINES:
            P.eng(e).wait(self.sCC, self.sCC.n)
            P.eng(e).wait(self.sDG, self.sDG.n)
            P.eng(e).wait(self.sDS, self.sDS.n)
            P.eng(e).wait(self.sCAST, self.sCAST.n)
            P.eng(e).wait(self.sCAST2, self.sCAST2.n)
        self.plans.append(("fin", P))


def kernel(**inputs):
    maps = _prepare_inputs(inputs)
    k = K()
    nc = k.build()
    res = run_bass_kernel_spmd(nc, maps, core_ids=list(range(NCORES)))
    return _assemble(res.results)
